# version 14
# baseline (speedup 1.0000x reference)
import numpy as np
from contextlib import ExitStack

import concourse.bass as bass
import concourse.mybir as mybir
from concourse.bass_utils import run_bass_kernel_spmd

F32 = mybir.dt.float32
BF16 = mybir.dt.bfloat16
F32R = mybir.dt.float32r
AX = mybir.AxisListType
AF = mybir.ActivationFunctionType
ALU = mybir.AluOpType

D = 4096
T = 2048
CH = 32
NH = 8
HE = 64
FQ = 512
NE = 4
RK = 16
ER = 64
MW = 68
TB = 256
PG = 1024
EPS = 1e-5
SQ512 = float(np.sqrt(512.0))
XW = 258
X1W = 1032


class Ctx:

    def __init__(self, nc, st):
        self.nc = nc
        self.sems = {}
        for n in ["tS", "vS", "sS", "gS", "dY", "dGw", "dGx0", "dGx1",
                  "dT0", "dT1", "dQ0", "dQ1", "dO0", "dO1", "dO2", "dO3"]:
            self.sems[n] = [st.enter_context(nc.semaphore(n)), 0]

    def inc(self, name, instr, n=1, account=None):
        s = self.sems[name]
        instr.then_inc(s[0], n)
        s[1] += n if account is None else account
        return instr

    def tok(self, name):
        return (name, self.sems[name][1])

    def val(self, name):
        return self.sems[name][1]

    def wait(self, eng, name, val=None):
        s = self.sems[name]
        v = s[1] if val is None else val
        if v > 0:
            eng.wait_ge(s[0], v)

    def wait_tok(self, eng, tok):
        if tok is not None and tok[1] > 0:
            eng.wait_ge(self.sems[tok[0]][0], tok[1])


def build_program(dbg=()):
    nc = bass.Bass("TRN2", target_bir_lowering=False, debug=False)

    xT = nc.dram_tensor("xT", [D, T], F32, kind="ExternalInput").ap()
    qtil = nc.dram_tensor("qtil", [4, 128, D], F32, kind="ExternalInput").ap()
    ktil = nc.dram_tensor("ktil", [4, 128, D], F32, kind="ExternalInput").ap()
    vtil = nc.dram_tensor("vtil", [4, 128, D], F32, kind="ExternalInput").ap()
    qb = nc.dram_tensor("qb", [FQ], F32, kind="ExternalInput").ap()
    kb = nc.dram_tensor("kb", [FQ], F32, kind="ExternalInput").ap()
    vb = nc.dram_tensor("vb", [FQ], F32, kind="ExternalInput").ap()
    wotil = nc.dram_tensor("wotil", [CH, 128, FQ], F32, kind="ExternalInput").ap()
    wob = nc.dram_tensor("wob", [D], F32, kind="ExternalInput").ap()
    atil = nc.dram_tensor("atil", [128, CH * MW], F32, kind="ExternalInput").ap()
    btil = nc.dram_tensor("btil", [ER, D], F32, kind="ExternalInput").ap()
    outT = nc.dram_tensor("outT", [D, T], F32, kind="ExternalOutput").ap()

    qT_im = nc.dram_tensor("qT_im", [4, 128, D], BF16, kind="Internal").ap()
    kT_im = nc.dram_tensor("kT_im", [4, 128, D], BF16, kind="Internal").ap()
    vT_im = nc.dram_tensor("vT_im", [4, 128, D], BF16, kind="Internal").ap()
    woT_im = nc.dram_tensor("woT_im", [CH, 128, FQ], BF16, kind="Internal").ap()
    x1_dram = nc.dram_tensor("x1_dram", [D, T], BF16, kind="Internal").ap()
    v_dram = nc.dram_tensor("v_dram", [FQ, T], BF16, kind="Internal").ap()
    q_dram = nc.dram_tensor("q_dram", [FQ, T], BF16, kind="Internal").ap()
    k_dram = nc.dram_tensor("k_dram", [FQ, T], BF16, kind="Internal").ap()

    dbg_t = {}
    for name in dbg:
        shp = {"x1": [D, T], "q": [FQ, T], "k": [FQ, T], "v": [FQ, T],
               "attn": [FQ, T], "sums": [NH, T], "mrow": [1, T],
               "rrow": [1, T], "qaug": [65, T], "maxs": [128, 16],
               "vsb": [128, 16 * 520], "probs": [128, PG],
               "attnu": [FQ, T], "vstg": [128, FQ]}[name]
        dt = F32 if name in ("sums", "rrow", "maxs") else BF16
        dbg_t[name] = nc.dram_tensor("dbg_" + name, shp, dt,
                                     kind="ExternalOutput").ap()

    base = ExitStack()
    with base:
        E = base.enter_context

        eyef = E(nc.sbuf_tensor("eyef", [128, 128], F32))
        eyeb = E(nc.sbuf_tensor("eyeb", [128, 128], BF16))
        onesf_row = E(nc.sbuf_tensor("onesf_row", [1, 128], F32))
        onesb_col = E(nc.sbuf_tensor("onesb_col", [128, 1], BF16))
        indf = E(nc.sbuf_tensor("indf", [NE, ER], F32))
        u0 = E(nc.sbuf_tensor("u0", [1, 128], BF16))
        u1 = E(nc.sbuf_tensor("u1", [1, 128], BF16))
        eps_col = E(nc.sbuf_tensor("eps_col", [128, 1], F32))
        cst = E(nc.sbuf_tensor("cst", [128, 2], F32))
        qb_col = E(nc.sbuf_tensor("qb_col", [128, 4], F32))
        kb_col = E(nc.sbuf_tensor("kb_col", [128, 4], F32))
        vb_col = E(nc.sbuf_tensor("vb_col", [128, 4], F32))
        wob_col = E(nc.sbuf_tensor("wob_col", [128, 32], F32))
        brow = E(nc.sbuf_tensor("brow", [1, 512], F32))
        wsA = E(nc.sbuf_tensor("wsA", [1, MW], BF16))
        wsQKV = E(nc.sbuf_tensor("wsQKV", [1, 3 * FQ], BF16))
        sx_col = E(nc.sbuf_tensor("sx_col", [128, 16], F32))
        ss_col = E(nc.sbuf_tensor("ss_col", [128, 16], F32))
        m_col = E(nc.sbuf_tensor("m_col", [128, 16], F32))
        r_col = E(nc.sbuf_tensor("r_col", [128, 16], F32))
        va_col = E(nc.sbuf_tensor("va_col", [128, 16], F32))
        vb2_col = E(nc.sbuf_tensor("vb2_col", [128, 16], F32))
        mrow = E(nc.sbuf_tensor("mrow", [1, T], BF16))
        rrow = E(nc.sbuf_tensor("rrow", [1, T], F32))
        rexp = E(nc.sbuf_tensor("rexp", [128, PG], F32))
        trash = E(nc.sbuf_tensor("trash", [128, 128], F32))
        mid_sb = E(nc.sbuf_tensor("mid_sb", [ER, TB], BF16))
        gm_sb = E(nc.sbuf_tensor("gm_sb", [ER, TB], BF16))
        lrow = E(nc.sbuf_tensor("lrow", [NE, TB], F32))
        ltok = E(nc.sbuf_tensor("ltok", [128, 2 * NE], F32))
        gsc = E(nc.sbuf_tensor("gsc", [128, 64], F32))
        wrow = E(nc.sbuf_tensor("wrow", [NE, TB], F32))
        tmp_f = E(nc.sbuf_tensor("tmp_f", [128, PG], F32))
        aextT = E(nc.sbuf_tensor("aextT", [128, CH * MW], BF16))
        bcatT = E(nc.sbuf_tensor("bcatT", [ER, D], BF16))
        x1g = E(nc.sbuf_tensor("x1g", [128, CH * X1W], BF16))
        attn = [E(nc.sbuf_tensor(f"attn{c}", [128, T], BF16)) for c in range(4)]
        sums_sb = [E(nc.sbuf_tensor(f"sums_sb{i}", [128, T], F32))
                   for i in range(2)]
        max_sb = E(nc.sbuf_tensor("max_sb", [128, 16], F32))
        PS = E(nc.psum_tensor("PS", [128, 4096], F32))

        block = E(nc.Block())
        ctx = Ctx(nc, base)
        nc.const_aps.register(F32, SQ512, cst[:, 0:1])
        nc.const_aps.register(F32, 2.0, cst[:, 1:2])

        def BK(b, c0=0, c1=512, r0=0, r1=128):
            return PS[r0:r1, 512 * b + c0: 512 * b + c1]

        bank_tok = [None] * 8

        def xb_sl(buf, dc, c0, n, tiles=None):
            return buf[:, dc * XW + c0: dc * XW + c0 + n]

        def strided_tok(buf, dc, chw, j0, ntile, wid=128):
            return buf[:, dc * chw + j0 * 129: dc * chw + (j0 + ntile) * 129] \
                .rearrange("p (a b) -> p a b", b=129)[:, :, 0:wid]

        def setup():
            def v0(v):
                v.memset(eyef[:], 1.0)
                v.memset(onesf_row[:], 1.0)
                v.memset(onesb_col[:], 1.0)
                v.memset(indf[:], 1.0)
                v.memset(eps_col[:], EPS)
                v.memset(cst[:, 0:1], SQ512)
                v.memset(cst[:, 1:2], 2.0)
                v.memset(u0[:, 0:64], 1.0)
                v.memset(u0[:, 64:128], 0.0)
                v.memset(u1[:, 0:64], 0.0)
                v.memset(u1[:, 64:128], 1.0)
                ctx.inc("vS", v.memset(
                    x1g[:, 128: CH * X1W: 129], 1.0))
            block.vector(v0)

            def g0(g):
                ctx.wait(g, "vS")
                ctx.inc("gS", g.affine_select(
                    out=eyef[:], in_=eyef[:], pattern=[[1, 128]],
                    compare_op=ALU.is_equal, fill=0.0,
                    base=0, channel_multiplier=-1))
                ctx.wait(g, "gS")
                ctx.inc("gS", g.affine_select(
                    out=indf[:], in_=indf[:], pattern=[[1, ER]],
                    compare_op=ALU.is_ge, fill=0.0,
                    base=0, channel_multiplier=-RK))
                ctx.wait(g, "gS")
                ctx.inc("gS", g.affine_select(
                    out=indf[:], in_=indf[:], pattern=[[-1, ER]],
                    compare_op=ALU.is_ge, fill=0.0,
                    base=RK - 1, channel_multiplier=RK))
                for (dst, src) in [(qT_im, qtil), (kT_im, ktil),
                                   (vT_im, vtil), (woT_im, wotil)]:
                    ctx.inc("dGw", g.dma_start(out=dst, in_=src), 16)
                ctx.inc("dGw", g.dma_start(out=aextT[:], in_=atil), 16)
                ctx.inc("dGw", g.dma_start(out=bcatT[:], in_=btil), 16)
            block.gpsimd(g0)

            def v1(v):
                ctx.wait(v, "gS")
                ctx.inc("vS", v.tensor_copy(eyeb[:], eyef[:]))
            block.vector(v1)

            def bias_cols(src_row_ap, n, dst_col, dstoff):
                nchunk = n // 128
                def yf(sync):
                    ctx.wait(sync, "vS")
                    ctx.inc("dY", sync.dma_start(
                        out=brow[0:1, 0:min(n, 512)],
                        in_=src_row_ap), 16)
                block.sync(yf)

                def tf(t):
                    ctx.wait(t, "dY")
                    ctx.wait(t, "vS")
                    ctx.wait(t, "gS")
                    for c in range(nchunk):
                        ctx.inc("tS", t.matmul(
                            PS[0:128, 5 * 512 + c: 5 * 512 + c + 1],
                            brow[0:1, c * 128:(c + 1) * 128], eyef[0:1, 0:1],
                            start=(c == 0), stop=(c == nchunk - 1),
                            is_transpose=True))
                block.tensor(tf)

                def vf(v):
                    ctx.wait(v, "tS")
                    ctx.inc("vS", v.tensor_copy(
                        dst_col[:, dstoff:dstoff + nchunk],
                        PS[0:128, 5 * 512: 5 * 512 + nchunk]))
                block.vector(vf)

            bias_cols(qb[None, :], FQ, qb_col, 0)
            bias_cols(kb[None, :], FQ, kb_col, 0)
            bias_cols(vb[None, :], FQ, vb_col, 0)
            for half in range(8):
                bias_cols(wob[None, half * 512:(half + 1) * 512], 512,
                          wob_col, half * 4)

            def t1(t):
                ctx.wait(t, "dGw")
                for dc in range(CH):
                    mm = t.matmul(BK(6, 0, MW, r1=1),
                                  onesb_col[:], aextT[:, dc * MW:(dc + 1) * MW],
                                  start=(dc == 0), stop=(dc == CH - 1))
                ctx.inc("tS", mm)
            block.tensor(t1)

            def v2(v):
                ctx.wait(v, "tS")
                ctx.inc("vS", v.tensor_copy(wsA[:], BK(6, 0, MW, r1=1)))
            block.vector(v2)
        setup()
        snap0 = {n: sv[1] for n, sv in ctx.sems.items()}
        for e in ("tensor", "vector", "scalar", "gpsimd", "sync"):
            def bf0(eng, snap0=snap0):
                for n, v in snap0.items():
                    if v > 0:
                        eng.wait_ge(ctx.sems[n][0], v)
            getattr(block, e)(bf0)

        def ln_gram(src_of, g0, pre_wait=None):
            toks = {}
            def tf(t):
                if pre_wait is not None:
                    pre_wait(t)
                for i in range(2):
                    ctx.wait_tok(t, bank_tok[i])
                    for dc in range(CH):
                        mm = t.matmul(BK(i, 0, 129),
                                      src_of(dc, g0 + i, 128),
                                      src_of(dc, g0 + i, 129),
                                      start=(dc == 0), stop=(dc == CH - 1))
                    ctx.inc("tS", mm)
                    toks[i] = ctx.val("tS")
            block.tensor(tf)
            return toks

        def ln_statsV(g0, gtoks):
            def vf(v):
                for i in range(2):
                    ti = g0 + i
                    ctx.wait(v, "tS", gtoks[i])
                    v.tensor_mul(trash[:], BK(i, 0, 128), eyef[:])
                    v.drain()
                    v.reduce_sum(ss_col[:, ti:ti + 1], trash[:], axis=AX.X)
                    v.drain()
                    ctx.inc("vS", v.tensor_copy(sx_col[:, ti:ti + 1],
                                                BK(i, 128, 129)))
                    bank_tok[i] = ctx.tok("vS")
                sl = slice(g0, g0 + 2)
                v.drain()
                v.tensor_scalar_mul(m_col[:, sl], sx_col[:, sl], -1.0 / D)
                v.tensor_scalar_mul(va_col[:, sl], ss_col[:, sl], 1.0 / D)
                v.drain()
                v.tensor_mul(vb2_col[:, sl], m_col[:, sl], m_col[:, sl])
                v.drain()
                ctx.inc("vS", v.tensor_sub(va_col[:, sl], va_col[:, sl],
                                           vb2_col[:, sl]))
            block.vector(vf)
            return ctx.val("vS")

        def ln_lnexp(g0, vtok):
            def sf(s):
                ctx.wait(s, "vS", vtok)
                s.activation(out=vb2_col[:, g0:g0 + 2], in_=va_col[:, g0:g0 + 2],
                             func=AF.Ln, bias=eps_col[:, 0:1])
                s.drain()
                ctx.inc("sS", s.activation(out=r_col[:, g0:g0 + 2],
                                           in_=vb2_col[:, g0:g0 + 2],
                                           func=AF.Exp, scale=-0.5))
            block.scalar(sf)
            return ctx.val("sS")

        def ln_rowsT(g0, stok):
            b2 = bank_tok[2]
            b3 = bank_tok[3]
            def tf(t):
                ctx.wait(t, "sS", stok)
                ctx.wait_tok(t, b2)
                ctx.wait_tok(t, b3)
                for i in range(2):
                    t.matmul(PS[0:1, 2 * 512 + i * 128: 2 * 512 + (i + 1) * 128],
                             m_col[:, g0 + i:g0 + i + 1], eyef[:],
                             start=(i == 0), stop=(i == 1), is_transpose=True)
                for i in range(2):
                    mm = t.matmul(
                        PS[0:1, 3 * 512 + i * 128: 3 * 512 + (i + 1) * 128],
                        r_col[:, g0 + i:g0 + i + 1], eyef[:],
                        start=(i == 0), stop=(i == 1), is_transpose=True)
                ctx.inc("tS", mm)
            block.tensor(tf)
            return ctx.val("tS")

        def ln_rowsV(g0, rtok):
            def vf(v):
                ctx.wait(v, "tS", rtok)
                t0 = g0 * 128
                v.tensor_copy(mrow[0:1, t0:t0 + 256], PS[0:1, 2 * 512:2 * 512 + 256])
                ctx.inc("vS", v.tensor_copy(rrow[0:1, t0:t0 + 256],
                                            PS[0:1, 3 * 512:3 * 512 + 256]))
            block.vector(vf)
            bank_tok[2] = ctx.tok("vS")
            bank_tok[3] = ctx.tok("vS")
            return ctx.val("vS")

        def mid_main(tb, src_mm):
            def tf(t):
                ctx.wait_tok(t, bank_tok[5])
                for dc in range(CH):
                    t.matmul(BK(5, 0, TB, r1=MW),
                             aextT[:, dc * MW:(dc + 1) * MW],
                             src_mm(dc), start=(dc == 0), stop=False)
            block.tensor(tf)

        def ffn_gate_delta(tb, src_res, evac_s, evac_v, v_mrow):
            t0 = tb * TB
            g0 = tb * 2
            sv = ctx.val("sS")
            vv = ctx.val("vS")
            b2 = bank_tok[2]
            b4 = [bank_tok[4 + k] for k in range(4)]
            b5 = bank_tok[5]
            ttoks = {}

            def tf(t):
                ctx.wait(t, "vS", v_mrow)
                mm = t.matmul(BK(5, 0, TB, r1=MW), wsA[:],
                              mrow[0:1, t0:t0 + TB], start=False, stop=True)
                ctx.inc("tS", mm)
                ttoks["aug"] = ctx.val("tS")
                ctx.wait_tok(t, b2)
                ctx.wait(t, "vS", vv + 1)
                for i in range(2):
                    mm = t.matmul(
                        PS[0:128, 2 * 512 + i * 4: 2 * 512 + (i + 1) * 4],
                        lrow[0:NE, i * 128:(i + 1) * 128], eyef[0:NE, 0:NE],
                        start=(i == 0), stop=(i == 1), is_transpose=True)
                ctx.inc("tS", mm)
                ttoks["lT"] = ctx.val("tS")
                ctx.wait(t, "vS", vv + 3)
                for i in range(2):
                    mm = t.matmul(
                        PS[0:NE, 2 * 512 + 128 + i * 128:
                           2 * 512 + 128 + (i + 1) * 128],
                        gsc[:, 52 + i * 4:52 + (i + 1) * 4], eyef[:],
                        start=(i == 0), stop=(i == 1), is_transpose=True)
                ctx.inc("tS", mm)
                ttoks["wT"] = ctx.val("tS")
                ctx.wait(t, "vS", vv + 4)
                ctx.wait(t, "sS", sv + 1)
                mm = t.matmul(BK(5, 0, TB, r1=ER), indf[:], wrow[:],
                              start=True, stop=True)
                ctx.inc("tS", mm)
                ttoks["ind"] = ctx.val("tS")
                ctx.wait(t, "vS", vv + 5)
                for oc in range(CH):
                    bkk = 4 + oc % 4
                    if oc >= 4:
                        prev = oc - 4
                        if prev % 2 == 0:
                            ctx.wait(t, "sS", sv + 2 + prev // 2 + 1)
                        else:
                            ctx.wait(t, "vS", vv + 5 + prev // 2 + 1)
                    else:
                        ctx.wait_tok(t, b4[oc])
                    t.matmul(BK(bkk, 0, TB), bcatT[:, oc * 128:(oc + 1) * 128],
                             gm_sb[:], start=True, stop=False)
                    mm = t.matmul(BK(bkk, 0, TB), eyeb[:], src_res(oc),
                                  start=False, stop=True)
                    ctx.inc("tS", mm)
                    ttoks[("d", oc)] = ctx.val("tS")
            block.tensor(tf)

            def sf(s):
                ctx.wait(s, "tS", ttoks["aug"])
                ctx.inc("sS", s.copy(mid_sb[:], BK(5, 0, TB, r1=ER)))
                ctx.wait(s, "vS", vv + 2)
                ctx.inc("sS", s.activation(out=gsc[:, 24:32], in_=ltok[:],
                                           func=AF.Exp))
                for oc in range(0, CH, 2):
                    ctx.wait(s, "tS", ttoks[("d", oc)])
                    evac_s(s, oc, 4 + oc % 4)
            block.scalar(sf)
            assert ctx.val("sS") == sv + 2 + 16, (ctx.val("sS"), sv + 18)

            def vf(v):
                ctx.wait(v, "tS", ttoks["aug"])
                ctx.inc("vS", v.tensor_copy(lrow[:], PS[ER:MW, 5 * 512:5 * 512 + TB]))
                ctx.wait(v, "tS", ttoks["lT"])
                v.tensor_copy(ltok[:], PS[0:128, 2 * 512:2 * 512 + 8])
                v.drain()
                for i in range(2):
                    v.tensor_single_scalar(ltok[:, i * 4:(i + 1) * 4],
                                           ltok[:, i * 4:(i + 1) * 4],
                                           r_col[:, g0 + i:g0 + i + 1],
                                           ALU.mult)
                v.drain()
                l_ = [ltok[:, e: 2 * NE: NE] for e in range(NE)]
                v.tensor_max(gsc[:, 0:2], l_[0], l_[1])
                v.tensor_tensor(gsc[:, 2:4], l_[0], l_[1], ALU.min)
                v.tensor_max(gsc[:, 4:6], l_[2], l_[3])
                v.tensor_tensor(gsc[:, 6:8], l_[2], l_[3], ALU.min)
                v.drain()
                v.tensor_tensor(gsc[:, 8:10], gsc[:, 0:2], gsc[:, 4:6], ALU.min)
                v.tensor_max(gsc[:, 10:12], gsc[:, 2:4], gsc[:, 6:8])
                v.drain()
                v.tensor_max(gsc[:, 12:14], gsc[:, 8:10], gsc[:, 10:12])
                v.drain()
                for e in range(NE):
                    v.tensor_tensor(gsc[:, 16 + e: 24: NE], l_[e],
                                    gsc[:, 12:14], ALU.is_ge)
                ctx.inc("vS", v.drain())
                ctx.wait(v, "sS", sv + 2)
                v.tensor_mul(gsc[:, 32:40], gsc[:, 24:32], gsc[:, 16:24])
                v.drain()
                v.tensor_add(gsc[:, 40:42], gsc[:, 32: 40: NE], gsc[:, 33: 40: NE])
                v.drain()
                v.tensor_add(gsc[:, 42:44], gsc[:, 34: 40: NE], gsc[:, 35: 40: NE])
                v.drain()
                v.tensor_add(gsc[:, 40:42], gsc[:, 40:42], gsc[:, 42:44])
                v.drain()
                v.tensor_scalar_mul(gsc[:, 44:46], gsc[:, 40:42], 0.5)
                v.drain()
                v.reciprocal(gsc[:, 46:48], gsc[:, 44:46])
                v.drain()
                for i in range(2):
                    v.tensor_mul(gsc[:, 48 + i:49 + i], gsc[:, 46 + i:47 + i],
                                 r_col[:, g0 + i:g0 + i + 1])
                v.drain()
                for e in range(NE):
                    v.tensor_mul(gsc[:, 52 + e: 60: NE], gsc[:, 32 + e: 40: NE],
                                 gsc[:, 48:50])
                ctx.inc("vS", v.drain())
                ctx.wait(v, "tS", ttoks["wT"])
                ctx.inc("vS", v.tensor_copy(
                    wrow[:], PS[0:NE, 2 * 512 + 128: 2 * 512 + 128 + TB]))
                ctx.wait(v, "tS", ttoks["ind"])
                ctx.inc("vS", v.tensor_mul(gm_sb[:], mid_sb[:],
                                           BK(5, 0, TB, r1=ER)))
                for oc in range(1, CH, 2):
                    ctx.wait(v, "tS", ttoks[("d", oc)])
                    evac_v(v, oc, 4 + oc % 4)
            block.vector(vf)
            assert ctx.val("vS") == vv + 5 + 16, (ctx.val("vS"), vv + 21)

            bank_tok[2] = ("vS", vv + 4)
            bank_tok[5] = ("vS", vv + 5)
            for oc in range(CH - 4, CH):
                bkk = 4 + oc % 4
                if oc % 2 == 0:
                    bank_tok[bkk] = ("sS", sv + 2 + oc // 2 + 1)
                else:
                    bank_tok[bkk] = ("vS", vv + 5 + oc // 2 + 1)
            return ttoks

        p1 = ExitStack()
        xbuf = [p1.enter_context(nc.sbuf_tensor(f"xb{i}", [128, CH * XW], BF16))
                for i in range(2)]
        wt = [p1.enter_context(nc.sbuf_tensor(f"wt{i}", [128, D], BF16))
              for i in range(2)]
        vt_g = p1.enter_context(nc.sbuf_tensor("vt_g", [128, 4 * PG], BF16))
        qkst = [p1.enter_context(nc.sbuf_tensor(f"qkst{i}", [128, PG], BF16))
                for i in range(2)]

        xb_cons = [None, None]
        wt_cons = [None, None]
        qkst_cons = [None, None]

        def v_ones_init(v):
            for b in range(2):
                v.memset(xbuf[b][:, 128: CH * XW: 129], 1.0)
            ctx.inc("vS", v.drain())
        block.vector(v_ones_init)

        def emit_xload(tb):
            if tb >= 8:
                return
            b = tb % 2
            def gf(g, tb=tb, b=b):
                ctx.wait_tok(g, xb_cons[b])
                for a in range(2):
                    dst = xbuf[b].rearrange("p (dc w) -> p dc w", w=XW)[
                        :, :, a * 129: a * 129 + 128]
                    src = xT[:, tb * TB + a * 128: tb * TB + (a + 1) * 128] \
                        .rearrange("(dc p) t -> p dc t", p=128)
                    ctx.inc("dGx%d" % b, g.dma_start(out=dst, in_=src), 16)
            block.gpsimd(gf)

        x_ld = {}
        emit_xload(0)
        x_ld[0] = ctx.tok("dGx0")
        emit_xload(1)
        x_ld[1] = ctx.tok("dGx1")

        wim = [qT_im] * 4 + [kT_im] * 4 + [vT_im] * 4
        wt_ld = {}

        def emit_wtload(pg, fi):
            if fi >= 12:
                return
            b = fi % 2
            c = fi % 4
            def sf(s, pg=pg, fi=fi, b=b, c=c):
                ctx.wait_tok(s, wt_cons[b])
                ctx.inc("dT%d" % b, s.dma_start(out=wt[b][:], in_=wim[fi][c]), 16)
            block.scalar(sf)
            wt_ld[(pg, fi)] = ctx.tok("dT%d" % b)

        def p1_evacs(tb, j):
            def evac_s(s, oc, bkk, j=j):
                dst = x1g[:, oc * X1W + j * 258: oc * X1W + (j + 1) * 258] \
                    .rearrange("p (a b) -> p a b", b=129)[:, :, 0:128]
                ctx.wait(s, "dY")
                ctx.inc("sS", s.copy(dst, BK(bkk, 0, TB)
                                     .rearrange("p (a b) -> p a b", b=128)))
            def evac_v(v, oc, bkk, j=j):
                dst = x1g[:, oc * X1W + j * 258: oc * X1W + (j + 1) * 258] \
                    .rearrange("p (a b) -> p a b", b=129)[:, :, 0:128]
                ctx.wait(v, "dY")
                ctx.inc("vS", v.tensor_copy(dst, BK(bkk, 0, TB)
                                            .rearrange("p (a b) -> p a b", b=128)))
            return evac_s, evac_v

        def emit_gram2_finish(prev):
            vtok2 = ln_statsV(prev["g0"], prev["g2toks"])
            stok2 = ln_lnexp(prev["g0"], vtok2)
            rtok2 = ln_rowsT(prev["g0"], stok2)
            ln_rowsV(prev["g0"], rtok2)
            def yf(sync, tb=prev["tb"], j=prev["j"]):
                ctx.wait(sync, "sS")
                ctx.wait(sync, "vS")
                for a2 in range(2):
                    a = j * 2 + a2
                    src = x1g.rearrange("p (dc w) -> p dc w", w=X1W)[
                        :, :, a * 129: a * 129 + 128]
                    dst = x1_dram[:, tb * TB + a2 * 128:
                                  tb * TB + (a2 + 1) * 128].rearrange(
                        "(dc p) t -> p dc t", p=128)
                    ctx.inc("dY", sync.dma_start(out=dst, in_=src), 16)
            block.sync(yf)

        def mk_srcs(tb):
            xb = xbuf[tb % 2]
            def src_x(dc, ti, n, xb=xb, tb=tb):
                lo = dc * XW + (ti - tb * 2) * 129
                return xb[:, lo: lo + n]
            def src_mm(dc, xb=xb):
                return strided_tok(xb, dc, XW, 0, 2, 128)
            def src_res(oc, xb=xb):
                return strided_tok(xb, oc, XW, 0, 2, 128)
            def pre(t, tb=tb):
                ctx.wait_tok(t, x_ld[tb])
                if tb == 0:
                    ctx.wait(t, "vS")
            return src_x, src_mm, src_res, pre

        prev = None
        nxt = None
        for pg in range(2):
            for j in range(4):
                tb = pg * 4 + j
                src_x, src_mm, src_res, pre = mk_srcs(tb)
                if nxt is None:
                    g1 = ln_gram(src_x, tb * 2, pre_wait=pre)
                    mid_main(tb, src_mm)
                else:
                    g1 = nxt["g1"]
                    if not nxt["mid"]:
                        mid_main(tb, src_mm)
                vtok1 = ln_statsV(tb * 2, g1)
                stok1 = ln_lnexp(tb * 2, vtok1)
                if prev is not None:
                    def pre2(t, prev=prev):
                        ctx.wait_tok(t, prev["etok_s"])
                        ctx.wait_tok(t, prev["etok_v"])
                    g2 = ln_gram(prev["srcg2"], prev["g0"], pre_wait=pre2)
                    vtok2 = ln_statsV(prev["g0"], g2)
                    stok2 = ln_lnexp(prev["g0"], vtok2)
                rtok1 = ln_rowsT(tb * 2, stok1)
                v_mrow = ln_rowsV(tb * 2, rtok1)
                g1n = None
                if tb + 1 < 8:
                    src_xn, src_mmn, _, pren = mk_srcs(tb + 1)
                    g1n = ln_gram(src_xn, (tb + 1) * 2, pre_wait=pren)
                if prev is not None:
                    rtok2 = ln_rowsT(prev["g0"], stok2)
                    ln_rowsV(prev["g0"], rtok2)
                    def yf(sync, tbp=prev["tb"], jp=prev["j"]):
                        ctx.wait(sync, "sS")
                        ctx.wait(sync, "vS")
                        for a2 in range(2):
                            aa = jp * 2 + a2
                            src = x1g.rearrange("p (dc w) -> p dc w", w=X1W)[
                                :, :, aa * 129: aa * 129 + 128]
                            dst = x1_dram[:, tbp * TB + a2 * 128:
                                          tbp * TB + (a2 + 1) * 128].rearrange(
                                "(dc p) t -> p dc t", p=128)
                            ctx.inc("dY", sync.dma_start(out=dst, in_=src), 16)
                    block.sync(yf)
                ev_s, ev_v = p1_evacs(tb, j)
                ffn_gate_delta(tb, src_res, ev_s, ev_v, v_mrow)
                xb_cons[tb % 2] = ctx.tok("tS")
                mid_ahead = (tb + 1 < 8) and (j != 3)
                if mid_ahead:
                    mid_main(tb + 1, src_mmn)
                emit_xload(tb + 2)
                if tb + 2 < 8:
                    x_ld[tb + 2] = ctx.tok("dGx%d" % (tb % 2))

                def srcg2(dc, ti, n, pg=pg):
                    lo = dc * X1W + (ti - pg * 8) * 129
                    return x1g[:, lo: lo + n]
                prev = {"tb": tb, "j": j, "g0": tb * 2, "srcg2": srcg2,
                        "etok_s": ctx.tok("sS"), "etok_v": ctx.tok("vS")}
                nxt = ({"g1": g1n, "mid": mid_ahead}
                       if g1n is not None else None)
            def pre3(t, prev=prev):
                ctx.wait_tok(t, prev["etok_s"])
                ctx.wait_tok(t, prev["etok_v"])
            g2 = ln_gram(prev["srcg2"], prev["g0"], pre_wait=pre3)
            vtok2 = ln_statsV(prev["g0"], g2)
            stok2 = ln_lnexp(prev["g0"], vtok2)
            rtok2 = ln_rowsT(prev["g0"], stok2)
            ln_rowsV(prev["g0"], rtok2)
            def yft(sync, tbp=prev["tb"], jp=prev["j"]):
                ctx.wait(sync, "sS")
                ctx.wait(sync, "vS")
                for a2 in range(2):
                    aa = jp * 2 + a2
                    src = x1g.rearrange("p (dc w) -> p dc w", w=X1W)[
                        :, :, aa * 129: aa * 129 + 128]
                    dst = x1_dram[:, tbp * TB + a2 * 128:
                                  tbp * TB + (a2 + 1) * 128].rearrange(
                        "(dc p) t -> p dc t", p=128)
                    ctx.inc("dY", sync.dma_start(out=dst, in_=src), 16)
            block.sync(yft)
            prev = None

            def trx(t, pg=pg):
                ctx.wait(t, "vS")
                ctx.wait_tok(t, bank_tok[6])
                ctx.wait_tok(t, bank_tok[7])
                for jb in range(2):
                    ctx.inc("tS", t.matmul(
                        BK(6 + jb), onesf_row[:],
                        rrow[0:1, pg * PG + jb * 512: pg * PG + (jb + 1) * 512],
                        start=True, stop=True))
            block.tensor(trx)

            def vrx(v):
                ctx.wait(v, "tS")
                for jb in range(2):
                    last = v.tensor_copy(rexp[:, jb * 512:(jb + 1) * 512],
                                         BK(6 + jb))
                ctx.inc("vS", last)
            block.vector(vrx)
            bank_tok[6] = ctx.tok("vS")
            bank_tok[7] = ctx.tok("vS")

            emit_wtload(pg, 0)
            emit_wtload(pg, 1)
            for fi in range(12):
                kind = fi // 4
                c = fi % 4
                wtb = wt[fi % 2]
                pair = (4, 5) if fi % 2 == 0 else (6, 7)

                if pg == 0:
                    def tws(t, fi=fi, wtb=wtb):
                        ctx.wait_tok(t, wt_ld[(0, fi)])
                        ctx.wait_tok(t, bank_tok[2])
                        for dc in range(CH):
                            mm = t.matmul(BK(2, 0, 128, r1=1), onesb_col[:],
                                          wtb[:, dc * 128:(dc + 1) * 128],
                                          start=(dc == 0), stop=(dc == CH - 1))
                        ctx.inc("tS", mm)
                    block.tensor(tws)

                    def vws(v, fi=fi):
                        ctx.wait(v, "tS")
                        ctx.inc("vS", v.tensor_copy(
                            wsQKV[0:1, fi * 128:(fi + 1) * 128],
                            BK(2, 0, 128, r1=1)))
                    block.vector(vws)
                    bank_tok[2] = ctx.tok("vS")

                def tp(t, fi=fi, pg=pg, wtb=wtb, pair=pair):
                    ctx.wait_tok(t, wt_ld[(pg, fi)])
                    ctx.wait(t, "vS")
                    ctx.wait_tok(t, bank_tok[pair[0]])
                    ctx.wait_tok(t, bank_tok[pair[1]])
                    for dc in range(CH):
                        for jb in range(2):
                            t.matmul(BK(pair[jb]),
                                     wtb[:, dc * 128:(dc + 1) * 128],
                                     strided_tok(x1g, dc, X1W, jb * 4, 4, 128),
                                     start=(dc == 0), stop=False)
                    for jb in range(2):
                        ctx.inc("tS", t.matmul(
                            BK(pair[jb]), wsQKV[0:1, fi * 128:(fi + 1) * 128],
                            mrow[0:1, pg * PG + jb * 512: pg * PG + (jb + 1) * 512],
                            start=False, stop=True))
                block.tensor(tp)
                wt_cons[fi % 2] = ctx.tok("tS")

                emit_wtload(pg, fi + 2)

                def vp(v, fi=fi, pair=pair):
                    ctx.wait(v, "tS")
                    ctx.wait(v, "sS")
                    ctx.wait_tok(v, qkst_cons[fi % 2])
                    v.tensor_mul(tmp_f[:, 0:512], BK(pair[0]), rexp[:, 0:512])
                    v.drain()
                    ctx.inc("vS", v.tensor_mul(tmp_f[:, 512:1024], BK(pair[1]),
                                               rexp[:, 512:1024]))
                block.vector(vp)
                bank_tok[pair[0]] = ctx.tok("vS")
                bank_tok[pair[1]] = ctx.tok("vS")

                bcol = [qb_col, kb_col, vb_col][kind]
                def sp(s, fi=fi, c=c, kind=kind, bcol=bcol):
                    ctx.wait(s, "vS")
                    if kind == 2:
                        dstap = vt_g[:, c * PG:(c + 1) * PG]
                    else:
                        dstap = qkst[fi % 2][:]
                    ctx.inc("sS", s.activation(out=dstap, in_=tmp_f[:, 0:1024],
                                               func=AF.Identity,
                                               bias=bcol[:, c:c + 1]))
                block.scalar(sp)

                if kind != 2:
                    def yq(sync, fi=fi, c=c, kind=kind, pg=pg):
                        ctx.wait(sync, "sS")
                        dstd = (q_dram if kind == 0 else k_dram)
                        ctx.inc("dQ%d" % (fi % 2), sync.dma_start(
                            out=dstd[c * 128:(c + 1) * 128,
                                     pg * PG:(pg + 1) * PG],
                            in_=qkst[fi % 2][:]), 16)
                    block.sync(yq)
                    qkst_cons[fi % 2] = ctx.tok("dQ%d" % (fi % 2))

            def yv(sync, pg=pg):
                ctx.wait(sync, "sS")
                for fc in range(4):
                    ctx.inc("dY", sync.dma_start(
                        out=v_dram[fc * 128:(fc + 1) * 128,
                                   pg * PG:(pg + 1) * PG],
                        in_=vt_g[:, fc * PG:(fc + 1) * PG]), 16)
            block.sync(yv)

        if "x1" in dbg_t:
            def ydbg(sync):
                ctx.wait(sync, "sS")
                ctx.wait(sync, "vS")
                ctx.wait(sync, "dY")
                src = x1_dram
                ctx.inc("dY", sync.dma_start(out=dbg_t["x1"], in_=src), 16)
            block.sync(ydbg)
        for nm, dr in (("q", q_dram), ("k", k_dram), ("v", v_dram)):
            if nm in dbg_t:
                def ydbg2(sync, nm=nm, dr=dr):
                    ctx.wait(sync, "dY")
                    ctx.wait(sync, "dQ0")
                    ctx.wait(sync, "dQ1")
                    ctx.inc("dY", sync.dma_start(out=dbg_t[nm], in_=dr), 16)
                block.sync(ydbg2)
        if "mrow" in dbg_t:
            def ydbg3(sync):
                ctx.wait(sync, "vS")
                ctx.inc("dY", sync.dma_start(out=dbg_t["mrow"], in_=mrow[:]), 16)
            block.sync(ydbg3)
        if "rrow" in dbg_t:
            def ydbg4(sync):
                ctx.wait(sync, "vS")
                ctx.inc("dY", sync.dma_start(out=dbg_t["rrow"], in_=rrow[:]), 16)
            block.sync(ydbg4)

        snap = {n: s[1] for n, s in ctx.sems.items()}
        for e in ("tensor", "vector", "scalar", "gpsimd", "sync"):
            def bf(eng, snap=snap):
                for n, v in snap.items():
                    if v > 0:
                        eng.wait_ge(ctx.sems[n][0], v)
            getattr(block, e)(bf)
        p1.close()

        p2 = ExitStack()
        v_sb = p2.enter_context(nc.sbuf_tensor("v_sb", [128, 16 * 640], BF16))
        Qaug = p2.enter_context(nc.sbuf_tensor("Qaug", [65, T], BF16))
        Kaug = p2.enter_context(nc.sbuf_tensor("Kaug", [65, T], BF16))
        probs = [p2.enter_context(nc.sbuf_tensor(f"probs{i}", [128, PG], BF16))
                 for i in range(2)]
        vstage = [p2.enter_context(nc.sbuf_tensor(f"vstage{i}", [128, FQ], BF16))
                  for i in range(2)]
        rstg = [p2.enter_context(nc.sbuf_tensor(f"rstg{i}", [1, T], BF16))
                for i in range(2)]
        mx2 = p2.enter_context(nc.sbuf_tensor("mx2", [128, 2], F32))

        def p2v0(v):
            for kt in range(16):
                for h in range(NH):
                    v.memset(v_sb[:, kt * 640 + h * 80 + 64:
                                  kt * 640 + h * 80 + 65], 1.0)
            v.memset(Kaug[64:65, :], -1.0)
            ctx.inc("vS", v.drain())
        block.vector(p2v0)

        vstage_cons = [None, None]
        for kt in range(16):
            def p2s0(s, kt=kt):
                if kt < 2:
                    ctx.wait(s, "dY")
                    ctx.wait(s, "vS")
                ctx.wait_tok(s, vstage_cons[kt % 2])
                ctx.inc("dT%d" % (kt % 2), s.dma_start(
                    out=vstage[kt % 2][:],
                    in_=v_dram[:, kt * 128:(kt + 1) * 128],
                    transpose=True), 16)
            block.scalar(p2s0)

            def p2v1(v, kt=kt):
                ctx.wait(v, "dT%d" % (kt % 2))
                dst = v_sb[:, kt * 640:(kt + 1) * 640].rearrange(
                    "p (h e) -> p h e", h=NH)[:, :, 0:64]
                src = vstage[kt % 2].rearrange("p (h e) -> p h e", h=NH)
                ctx.inc("vS", v.tensor_copy(dst, src))
            block.vector(p2v1)
            vstage_cons[kt % 2] = ctx.tok("vS")

        QA = [Qaug, p2.enter_context(nc.sbuf_tensor("Qaug1", [65, T], BF16))]
        KA = [Kaug, p2.enter_context(nc.sbuf_tensor("Kaug1", [65, T], BF16))]

        def p2v0b(v):
            ctx.inc("vS", v.memset(KA[1][64:65, :], -1.0))
        block.vector(p2v0b)

        ttok_prev = {}
        qk_ld = [None, None]
        for i in range(NH + 1):
            hA = i if i < NH else None
            hB = i - 1 if i >= 1 else None
            bi = i % 2
            bj = (i - 1) % 2

            if hA is not None:
                def ya(sync, hA=hA, bi=bi, tprev=dict(ttok_prev)):
                    if ("B", 31) in tprev:
                        ctx.wait(sync, "tS", tprev[("B", 31)])
                    else:
                        ctx.wait(sync, "tS")
                    ctx.inc("dY", sync.dma_start(
                        out=QA[bi][0:64, :],
                        in_=q_dram[hA * 64:(hA + 1) * 64, :]), 16)
                    ctx.inc("dY", sync.dma_start(
                        out=KA[bi][0:64, :],
                        in_=k_dram[hA * 64:(hA + 1) * 64, :]), 16)
                block.sync(ya)
                qk_ld[bi] = ctx.tok("dY")

            vbase = ctx.val("vS")
            vtok = {}
            c = vbase
            if hA is not None:
                for q in range(8):
                    c += 1; vtok[("rm1", q)] = c
                    c += 1; vtok[("rm2", q)] = c
            if hB is not None:
                c += 1; vtok[("vo", 0)] = c
                c += 1; vtok[("so", 0)] = c
            if hA is not None:
                for q in range(8, 16):
                    c += 1; vtok[("rm1", q)] = c
                    c += 1; vtok[("rm2", q)] = c
                c += 1; vtok[("vc",)] = c
            if hB is not None:
                c += 1; vtok[("vo", 1)] = c
                c += 1; vtok[("so", 1)] = c
            vend = c
            sbase = ctx.val("sS")
            stok = {j: sbase + j + 1 for j in range(32)} if hB is not None else {}
            send = sbase + (32 if hB is not None else 0)

            ttok = {}

            def tbody(t, hA=hA, hB=hB, bi=bi, bj=bj, vtok=vtok, stok=stok,
                      vbase=vbase, ttok=ttok, qk=dict(q0=qk_ld[bi])):
                for step in range(16):
                    if step == 0:
                        ctx.wait(t, "vS", vbase)
                    if hA is not None:
                        if step == 0:
                            ctx.wait_tok(t, qk["q0"])
                        else:
                            ctx.wait(t, "vS", vtok[("rm2", step - 1)])
                        for kb_ in range(2):
                            mm = t.matmul(BK(kb_),
                                          QA[bi][0:64, step * 128:(step + 1) * 128],
                                          KA[bi][0:64, kb_ * 512:(kb_ + 1) * 512],
                                          start=True, stop=True)
                        ctx.inc("tS", mm)
                        ttok[("A1", step)] = ctx.val("tS")
                    for half in range(2):
                        if hB is not None:
                            jj = 2 * step + half
                            qh, kt = jj // 16, jj % 16
                            pr = 4 + (jj % 2) * 2
                            if jj >= 2:
                                ctx.wait(t, "sS", stok[jj - 2])
                            if qh == 1 and kt == 0:
                                ctx.wait(t, "vS", vtok[("so", 0)])
                            for js in range(2):
                                mm = t.matmul(
                                    BK(pr + js),
                                    KA[bj][0:65, kt * 128:(kt + 1) * 128],
                                    QA[bj][0:65, qh * PG + js * 512:
                                           qh * PG + (js + 1) * 512],
                                    start=True, stop=True)
                            ctx.inc("tS", mm)
                            ttok[("B", jj)] = ctx.val("tS")
                            ctx.wait(t, "sS", stok[jj])
                            for js in range(2):
                                mm = t.matmul(
                                    PS[0:65, (2 + js) * 512:(2 + js) * 512 + 512],
                                    v_sb[:, kt * 640 + hB * 80:
                                         kt * 640 + hB * 80 + 65],
                                    probs[jj % 2][:, js * 512:(js + 1) * 512],
                                    start=(kt == 0), stop=(kt == 15))
                            ctx.inc("tS", mm)
                            ttok[("AV", jj)] = ctx.val("tS")
                        if hA is not None and half == 0:
                            ctx.wait(t, "vS", vtok[("rm1", step)])
                            for kb_ in range(2):
                                mm = t.matmul(
                                    BK(kb_),
                                    QA[bi][0:64, step * 128:(step + 1) * 128],
                                    KA[bi][0:64, 1024 + kb_ * 512:
                                           1024 + (kb_ + 1) * 512],
                                    start=True, stop=True)
                            ctx.inc("tS", mm)
                            ttok[("A2", step)] = ctx.val("tS")
                if hA is not None:
                    ctx.wait(t, "vS", vtok[("rm2", 15)])
                    if hB is not None:
                        ctx.wait(t, "sS", stok[31])
                    for qt in range(16):
                        bkk = 4 + qt // 4
                        mm = t.matmul(
                            PS[0:1, bkk * 512 + (qt % 4) * 128:
                               bkk * 512 + (qt % 4 + 1) * 128],
                            max_sb[:, qt:qt + 1], eyef[:],
                            start=(qt % 4 == 0), stop=(qt % 4 == 3),
                            is_transpose=True)
                        if qt % 4 == 3:
                            ctx.inc("tS", mm)
                    ttok[("cT",)] = ctx.val("tS")
            block.tensor(tbody)

            if hB is not None:
                def sbody(s, hB=hB, ttok=ttok, stok=stok):
                    for jj in range(32):
                        ctx.wait(s, "tS", ttok[("B", jj)])
                        if jj >= 2:
                            ctx.wait(s, "tS", ttok[("AV", jj - 2)])
                        pr = 4 + (jj % 2) * 2
                        ctx.inc("sS", s.activation(
                            out=probs[jj % 2][:],
                            in_=PS[0:128, pr * 512:pr * 512 + 1024],
                            func=AF.Exp))
                block.scalar(sbody)
                assert ctx.val("sS") == send

            def vbody(v, hA=hA, hB=hB, bi=bi, ttok=ttok):
                def rmq(q):
                    ctx.wait(v, "tS", ttok[("A1", q)])
                    v.reduce_max(mx2[:, 0:1], PS[0:128, 0:1024], axis=AX.X)
                    ctx.inc("vS", v.drain())
                    ctx.wait(v, "tS", ttok[("A2", q)])
                    v.reduce_max(mx2[:, 1:2], PS[0:128, 0:1024], axis=AX.X)
                    v.drain()
                    ctx.inc("vS", v.tensor_max(max_sb[:, q:q + 1],
                                               mx2[:, 0:1], mx2[:, 1:2]))
                if hA is not None:
                    for q in range(8):
                        rmq(q)
                if hB is not None:
                    ctx.wait(v, "tS", ttok[("AV", 15)])
                    r0 = (hB % 2) * 64
                    ctx.inc("vS", v.tensor_copy(
                        attn[hB // 2][r0:r0 + 64, 0:PG],
                        PS[0:64, 2 * 512: 2 * 512 + 1024]))
                    r1 = (hB % 4) * 32
                    ctx.inc("vS", v.tensor_scalar_mul(
                        sums_sb[hB // 4][r1:r1 + 1, 0:PG],
                        PS[64:65, 2 * 512: 2 * 512 + 1024], SQ512))
                if hA is not None:
                    for q in range(8, 16):
                        rmq(q)
                    ctx.wait(v, "tS", ttok[("cT",)])
                    for g4 in range(4):
                        last = v.tensor_copy(
                            QA[bi][64:65, g4 * 512:(g4 + 1) * 512],
                            PS[0:1, (4 + g4) * 512:(4 + g4) * 512 + 512])
                    ctx.inc("vS", last)
                if hB is not None:
                    ctx.wait(v, "tS", ttok[("AV", 31)])
                    r0 = (hB % 2) * 64
                    ctx.inc("vS", v.tensor_copy(
                        attn[hB // 2][r0:r0 + 64, PG:2 * PG],
                        PS[0:64, 2 * 512: 2 * 512 + 1024]))
                    r1 = (hB % 4) * 32
                    ctx.inc("vS", v.tensor_scalar_mul(
                        sums_sb[hB // 4][r1:r1 + 1, PG:2 * PG],
                        PS[64:65, 2 * 512: 2 * 512 + 1024], SQ512))
            block.vector(vbody)
            assert ctx.val("vS") == vend, (ctx.val("vS"), vend)
            ttok_prev = ttok
            ttok_prev = ttok

        def vr(v):
            ctx.wait(v, "vS")
            ctx.wait(v, "sS")
            ctx.wait(v, "dY")
            ctx.inc("vS", v.drain())
        block.vector(vr)

        for c in range(4):
            def vstg(v, c=c):
                ctx.wait(v, "vS")
                h0, h1 = 2 * c, 2 * c + 1
                v.tensor_copy(
                    rstg[0][:],
                    sums_sb[h0 // 4][(h0 % 4) * 32:(h0 % 4) * 32 + 1, :])
                ctx.inc("vS", v.tensor_copy(
                    rstg[1][:],
                    sums_sb[h1 // 4][(h1 % 4) * 32:(h1 % 4) * 32 + 1, :]))
            block.vector(vstg)
            for jb in range(4):
                def te(t, c=c, jb=jb):
                    ctx.wait(t, "vS")
                    bkk = jb % 2
                    t.matmul(BK(bkk), u0[:],
                             rstg[0][0:1, jb * 512:(jb + 1) * 512],
                             start=True, stop=False)
                    ctx.inc("tS", t.matmul(
                        BK(bkk), u1[:],
                        rstg[1][0:1, jb * 512:(jb + 1) * 512],
                        start=False, stop=True))
                block.tensor(te)

                def ve(v, c=c, jb=jb):
                    ctx.wait(v, "tS")
                    bkk = jb % 2
                    v.reciprocal(tmp_f[:, 0:512], BK(bkk))
                    v.drain()
                    ctx.inc("vS", v.tensor_mul(
                        attn[c][:, jb * 512:(jb + 1) * 512],
                        attn[c][:, jb * 512:(jb + 1) * 512],
                        tmp_f[:, 0:512]))
                block.vector(ve)

        if "probs" in dbg_t:
            def ypr(sync):
                ctx.wait(sync, "sS")
                ctx.wait(sync, "tS")
                ctx.inc("dY", sync.dma_start(out=dbg_t["probs"],
                                             in_=probs[1][:]), 16)
            block.sync(ypr)
        if "qaug" in dbg_t:
            def yqa(sync):
                ctx.wait(sync, "vS")
                ctx.wait(sync, "tS")
                ctx.inc("dY", sync.dma_start(out=dbg_t["qaug"], in_=Qaug[:]), 16)
            block.sync(yqa)
        if "maxs" in dbg_t:
            def yma(sync):
                ctx.wait(sync, "vS")
                ctx.inc("dY", sync.dma_start(out=dbg_t["maxs"], in_=max_sb[:]), 16)
            block.sync(yma)
        if "vstg" in dbg_t:
            def yvg(sync):
                ctx.wait(sync, "vS")
                ctx.wait(sync, "dT0")
                ctx.inc("dY", sync.dma_start(out=dbg_t["vstg"],
                                             in_=vstage[1][:]), 16)
            block.sync(yvg)
        if "vsb" in dbg_t:
            def yvs(sync):
                ctx.wait(sync, "vS")
                ctx.wait(sync, "dT0")
                src = v_sb.rearrange("p (kt h e) -> p kt h e", kt=16, h=NH)[
                    :, :, :, 0:65]
                ctx.inc("dY", sync.dma_start(out=dbg_t["vsb"], in_=src), 16)
            block.sync(yvs)
        if "attn" in dbg_t:
            def yat(sync):
                ctx.wait(sync, "vS")
                for c in range(4):
                    ctx.inc("dY", sync.dma_start(
                        out=dbg_t["attn"][c * 128:(c + 1) * 128, :],
                        in_=attn[c][:]), 16)
            block.sync(yat)
        if "sums" in dbg_t:
            def ysm(sync):
                ctx.wait(sync, "sS")
                for h in range(NH):
                    ctx.inc("dY", sync.dma_start(
                        out=dbg_t["sums"][h:h + 1, :],
                        in_=sums_sb[h // 4][(h % 4) * 32:(h % 4) * 32 + 1, :]), 16)
            block.sync(ysm)

        snap = {n: s[1] for n, s in ctx.sems.items()}
        for e in ("tensor", "vector", "scalar", "gpsimd", "sync"):
            def bf2(eng, snap=snap):
                for n, v in snap.items():
                    if v > 0:
                        eng.wait_ge(ctx.sems[n][0], v)
            getattr(block, e)(bf2)
        p2.close()
        for b in range(8):
            bank_tok[b] = None

        p3 = ExitStack()
        wot = [p3.enter_context(nc.sbuf_tensor(f"wot{i}", [128, FQ], BF16))
               for i in range(2)]
        outst = [p3.enter_context(nc.sbuf_tensor(f"outst{i}", [128, TB], F32))
                 for i in range(4)]
        outst_cons = [None] * 4
        wot_cons = [None, None]
        wot_ld = {}

        def emit_wotload(pg, dm):
            if dm >= CH:
                return
            b = dm % 2
            def sf(s, pg=pg, dm=dm, b=b):
                ctx.wait_tok(s, wot_cons[b])
                ctx.inc("dT%d" % b, s.dma_start(out=wot[b][:], in_=woT_im[dm]), 16)
            block.scalar(sf)
            wot_ld[(pg, dm)] = ctx.tok("dT%d" % b)

        for pg in range(2):
            def yr(sync, pg=pg):
                ctx.wait(sync, "tS")
                ctx.wait(sync, "sS")
                for a in range(8):
                    dst = x1g.rearrange("p (dc w) -> p dc w", w=X1W)[
                        :, :, a * 129: a * 129 + 128]
                    src = x1_dram[:, pg * PG + a * 128:
                                  pg * PG + (a + 1) * 128].rearrange(
                        "(dc p) t -> p dc t", p=128)
                    ctx.inc("dY", sync.dma_start(out=dst, in_=src), 16)
            block.sync(yr)
            x1_ld = ctx.tok("dY")

            emit_wotload(pg, 0)
            emit_wotload(pg, 1)
            for dm in range(CH):
                pair = (0, 1) if dm % 2 == 0 else (2, 3)
                def tw2(t, pg=pg, dm=dm, pair=pair):
                    ctx.wait_tok(t, wot_ld[(pg, dm)])
                    ctx.wait_tok(t, x1_ld)
                    ctx.wait_tok(t, bank_tok[pair[0]])
                    ctx.wait_tok(t, bank_tok[pair[1]])
                    for fc in range(4):
                        for js in range(2):
                            t.matmul(BK(pair[js]),
                                     wot[dm % 2][:, fc * 128:(fc + 1) * 128],
                                     attn[fc][:, pg * PG + js * 512:
                                               pg * PG + (js + 1) * 512],
                                     start=(fc == 0), stop=False)
                    for js in range(2):
                        ctx.inc("tS", t.matmul(
                            BK(pair[js]), eyeb[:],
                            strided_tok(x1g, dm, X1W, js * 4, 4, 128),
                            start=False, stop=True))
                block.tensor(tw2)
                wot_cons[dm % 2] = ctx.tok("tS")
                emit_wotload(pg, dm + 2)

                def sw2(s, dm=dm, pair=pair):
                    ctx.wait(s, "tS")
                    dst = x1g[:, dm * X1W: dm * X1W + 8 * 129].rearrange(
                        "p (a b) -> p a b", b=129)[:, :, 0:128]
                    ctx.inc("sS", s.activation(
                        out=dst, in_=PS[0:128, pair[0] * 512: pair[0] * 512 + 1024]
                        .rearrange("p (a b) -> p a b", b=128),
                        func=AF.Identity, bias=wob_col[:, dm:dm + 1]))
                block.scalar(sw2)
                bank_tok[pair[0]] = ctx.tok("sS")
                bank_tok[pair[1]] = ctx.tok("sS")

            if "x2" in dbg_t and pg == 0:
                pass

            for j in range(4):
                tb = pg * 4 + j

                def src_x2(dc, ti, n, pg=pg):
                    lo = dc * X1W + (ti - pg * 8) * 129
                    return x1g[:, lo: lo + n]

                def src_mm2(dc, j=j):
                    return strided_tok(x1g, dc, X1W, j * 2, 2, 128)

                def src_res2(oc, j=j):
                    return strided_tok(x1g, oc, X1W, j * 2, 2, 128)

                def pre(t):
                    ctx.wait(t, "sS")
                g3 = ln_gram(src_x2, tb * 2, pre_wait=pre if j == 0 else None)
                mid_main(tb, src_mm2)
                vtok = ln_statsV(tb * 2, g3)
                stok = ln_lnexp(tb * 2, vtok)
                rtok = ln_rowsT(tb * 2, stok)
                v_mrow = ln_rowsV(tb * 2, rtok)

                dO_base = [ctx.val("dO%d" % ob) for ob in range(4)]
                etoks = {}

                def evac_s(s, oc, bkk, dO_base=dO_base, etoks=etoks):
                    ob = oc % 4
                    want = dO_base[ob] + 16 * (oc // 4)
                    if want > 0:
                        ctx.wait(s, "dO%d" % ob, want)
                    ctx.inc("sS", s.copy(outst[ob][:], BK(bkk, 0, TB)))
                    etoks[oc] = ctx.tok("sS")

                def evac_v(v, oc, bkk, dO_base=dO_base, etoks=etoks):
                    ob = oc % 4
                    want = dO_base[ob] + 16 * (oc // 4)
                    if want > 0:
                        ctx.wait(v, "dO%d" % ob, want)
                    ctx.inc("vS", v.tensor_copy(outst[ob][:], BK(bkk, 0, TB)))
                    etoks[oc] = ctx.tok("vS")

                ffn_gate_delta(tb, src_res2, evac_s, evac_v, v_mrow)

                def yout(sync, tb=tb, etoks=etoks):
                    for oc in range(CH):
                        ctx.wait_tok(sync, etoks[oc])
                        ctx.inc("dO%d" % (oc % 4), sync.dma_start(
                            out=outT[oc * 128:(oc + 1) * 128,
                                     tb * TB:(tb + 1) * TB],
                            in_=outst[oc % 4][:]), 16)
                block.sync(yout)
        def yfin(sync):
            ctx.wait(sync, "dY")
            for ob in range(4):
                ctx.wait(sync, "dO%d" % ob)
        block.sync(yfin)
        p3.close()

    return nc


_NC_CACHE = {}


def _get_nc(dbg=()):
    key = tuple(sorted(dbg))
    if key not in _NC_CACHE:
        _NC_CACHE[key] = build_program(dbg=key)
    return _NC_CACHE[key]


def _tile_w(w):
    fo, d = w.shape
    return np.ascontiguousarray(
        w.reshape(fo // 128, 128, d // 128, 128).transpose(0, 3, 2, 1)
        .reshape(fo // 128, 128, d))


def make_inmaps(L0_attn_x, L0_attn_Wqk_w, L0_attn_Wqk_b, L0_attn_Wv_w,
                L0_attn_Wv_b, L0_attn_Wo_w, L0_attn_Wo_b, L0_ffn_router_w,
                L0_ffn_A, L0_ffn_B):
    x = np.asarray(L0_attn_x, np.float32)
    wqk = np.asarray(L0_attn_Wqk_w, np.float32)
    wqkb = np.asarray(L0_attn_Wqk_b, np.float32)
    aext = np.concatenate(
        [np.asarray(L0_ffn_A, np.float32).reshape(ER, D),
         np.asarray(L0_ffn_router_w, np.float32)], axis=0)
    atil = np.ascontiguousarray(
        aext.reshape(MW, CH, 128).transpose(2, 1, 0).reshape(128, CH * MW))
    btil = np.ascontiguousarray(
        np.asarray(L0_ffn_B, np.float32).transpose(0, 2, 1).reshape(ER, D))
    shared = {
        "qtil": _tile_w(wqk[0::2]),
        "ktil": _tile_w(wqk[1::2]),
        "vtil": _tile_w(np.asarray(L0_attn_Wv_w, np.float32)),
        "qb": np.ascontiguousarray(wqkb[0::2]),
        "kb": np.ascontiguousarray(wqkb[1::2]),
        "vb": np.ascontiguousarray(np.asarray(L0_attn_Wv_b, np.float32)),
        "wotil": _tile_w(np.asarray(L0_attn_Wo_w, np.float32)),
        "wob": np.ascontiguousarray(np.asarray(L0_attn_Wo_b, np.float32)),
        "atil": atil,
        "btil": btil,
    }
    return [dict(shared, xT=np.ascontiguousarray(x[b].T)) for b in range(8)]


def kernel(**inputs):
    in_maps = make_inmaps(**inputs)
    nc = _get_nc()
    res = run_bass_kernel_spmd(nc, in_maps, core_ids=list(range(8)))
    return np.stack([np.asarray(r["outT"]).T for r in res.results]).astype(np.float32)


# version 16
# speedup vs baseline: 1.0162x; 1.0162x over previous
import numpy as np
from contextlib import ExitStack

import concourse.bass as bass
import concourse.mybir as mybir
from concourse.bass_utils import run_bass_kernel_spmd

F32 = mybir.dt.float32
BF16 = mybir.dt.bfloat16
F32R = mybir.dt.float32r
AX = mybir.AxisListType
AF = mybir.ActivationFunctionType
ALU = mybir.AluOpType

D = 4096
T = 2048
CH = 32
NH = 8
HE = 64
FQ = 512
NE = 4
RK = 16
ER = 64
MW = 68
TB = 256
PG = 1024
EPS = 1e-5
SQ512 = float(np.sqrt(512.0))
XW = 258
X1W = 1032


class Ctx:

    def __init__(self, nc, st):
        self.nc = nc
        self.sems = {}
        for n in ["tS", "vS", "sS", "gS", "dY", "dGw", "dGx0", "dGx1",
                  "dT0", "dT1", "dQ0", "dQ1", "dO0", "dO1", "dO2", "dO3"]:
            self.sems[n] = [st.enter_context(nc.semaphore(n)), 0]

    def inc(self, name, instr, n=1, account=None):
        s = self.sems[name]
        instr.then_inc(s[0], n)
        s[1] += n if account is None else account
        return instr

    def tok(self, name):
        return (name, self.sems[name][1])

    def val(self, name):
        return self.sems[name][1]

    def wait(self, eng, name, val=None):
        s = self.sems[name]
        v = s[1] if val is None else val
        if v > 0:
            eng.wait_ge(s[0], v)

    def wait_tok(self, eng, tok):
        if tok is not None and tok[1] > 0:
            eng.wait_ge(self.sems[tok[0]][0], tok[1])


def build_program(dbg=()):
    nc = bass.Bass("TRN2", target_bir_lowering=False, debug=False)

    xT = nc.dram_tensor("xT", [D, T], F32, kind="ExternalInput").ap()
    qtil = nc.dram_tensor("qtil", [4, 128, D], F32, kind="ExternalInput").ap()
    ktil = nc.dram_tensor("ktil", [4, 128, D], F32, kind="ExternalInput").ap()
    vtil = nc.dram_tensor("vtil", [4, 128, D], F32, kind="ExternalInput").ap()
    qb = nc.dram_tensor("qb", [FQ], F32, kind="ExternalInput").ap()
    kb = nc.dram_tensor("kb", [FQ], F32, kind="ExternalInput").ap()
    vb = nc.dram_tensor("vb", [FQ], F32, kind="ExternalInput").ap()
    wotil = nc.dram_tensor("wotil", [CH, 128, FQ], F32, kind="ExternalInput").ap()
    wob = nc.dram_tensor("wob", [D], F32, kind="ExternalInput").ap()
    atil = nc.dram_tensor("atil", [128, CH * MW], F32, kind="ExternalInput").ap()
    btil = nc.dram_tensor("btil", [ER, D], F32, kind="ExternalInput").ap()
    outT = nc.dram_tensor("outT", [D, T], F32, kind="ExternalOutput").ap()

    qT_im = nc.dram_tensor("qT_im", [4, 128, D], BF16, kind="Internal").ap()
    kT_im = nc.dram_tensor("kT_im", [4, 128, D], BF16, kind="Internal").ap()
    vT_im = nc.dram_tensor("vT_im", [4, 128, D], BF16, kind="Internal").ap()
    woT_im = nc.dram_tensor("woT_im", [CH, 128, FQ], BF16, kind="Internal").ap()
    x1_dram = nc.dram_tensor("x1_dram", [D, T], BF16, kind="Internal").ap()
    v_dram = nc.dram_tensor("v_dram", [FQ, T], BF16, kind="Internal").ap()
    q_dram = nc.dram_tensor("q_dram", [FQ, T], BF16, kind="Internal").ap()
    k_dram = nc.dram_tensor("k_dram", [FQ, T], BF16, kind="Internal").ap()

    dbg_t = {}
    for name in dbg:
        shp = {"x1": [D, T], "q": [FQ, T], "k": [FQ, T], "v": [FQ, T],
               "attn": [FQ, T], "sums": [NH, T], "mrow": [1, T],
               "rrow": [1, T], "qaug": [65, T], "maxs": [128, 16],
               "vsb": [128, 16 * 520], "probs": [128, PG],
               "attnu": [FQ, T], "vstg": [128, FQ]}[name]
        dt = F32 if name in ("sums", "rrow", "maxs") else BF16
        dbg_t[name] = nc.dram_tensor("dbg_" + name, shp, dt,
                                     kind="ExternalOutput").ap()

    base = ExitStack()
    with base:
        E = base.enter_context

        eyef = E(nc.sbuf_tensor("eyef", [128, 128], F32))
        eyeb = E(nc.sbuf_tensor("eyeb", [128, 128], BF16))
        onesf_row = E(nc.sbuf_tensor("onesf_row", [1, 128], F32))
        onesb_col = E(nc.sbuf_tensor("onesb_col", [128, 1], BF16))
        indf = E(nc.sbuf_tensor("indf", [NE, ER], F32))
        u0 = E(nc.sbuf_tensor("u0", [1, 128], BF16))
        u1 = E(nc.sbuf_tensor("u1", [1, 128], BF16))
        eps_col = E(nc.sbuf_tensor("eps_col", [128, 1], F32))
        cst = E(nc.sbuf_tensor("cst", [128, 2], F32))
        qb_col = E(nc.sbuf_tensor("qb_col", [128, 4], F32))
        kb_col = E(nc.sbuf_tensor("kb_col", [128, 4], F32))
        vb_col = E(nc.sbuf_tensor("vb_col", [128, 4], F32))
        wob_col = E(nc.sbuf_tensor("wob_col", [128, 32], F32))
        brow = E(nc.sbuf_tensor("brow", [1, 512], F32))
        wsA = E(nc.sbuf_tensor("wsA", [1, MW], BF16))
        wsQKV = E(nc.sbuf_tensor("wsQKV", [1, 3 * FQ], BF16))
        sx_col = E(nc.sbuf_tensor("sx_col", [128, 16], F32))
        ss_col = E(nc.sbuf_tensor("ss_col", [128, 16], F32))
        m_col = E(nc.sbuf_tensor("m_col", [128, 16], F32))
        r_col = E(nc.sbuf_tensor("r_col", [128, 16], F32))
        va_col = E(nc.sbuf_tensor("va_col", [128, 16], F32))
        vb2_col = E(nc.sbuf_tensor("vb2_col", [128, 16], F32))
        mrow = E(nc.sbuf_tensor("mrow", [1, T], BF16))
        rrow = E(nc.sbuf_tensor("rrow", [1, T], F32))
        rexp = E(nc.sbuf_tensor("rexp", [128, PG], F32))
        trash = E(nc.sbuf_tensor("trash", [128, 128], F32))
        mid_sb = E(nc.sbuf_tensor("mid_sb", [ER, TB], BF16))
        gm_sb = E(nc.sbuf_tensor("gm_sb", [ER, TB], BF16))
        lrow = E(nc.sbuf_tensor("lrow", [NE, TB], F32))
        ltok = E(nc.sbuf_tensor("ltok", [128, 2 * NE], F32))
        gsc = E(nc.sbuf_tensor("gsc", [128, 64], F32))
        wrow = E(nc.sbuf_tensor("wrow", [NE, TB], F32))
        tmp_f = E(nc.sbuf_tensor("tmp_f", [128, PG], F32))
        aextT = E(nc.sbuf_tensor("aextT", [128, CH * MW], BF16))
        bcatT = E(nc.sbuf_tensor("bcatT", [ER, D], BF16))
        x1g = E(nc.sbuf_tensor("x1g", [128, CH * X1W], BF16))
        attn = [E(nc.sbuf_tensor(f"attn{c}", [128, T], BF16)) for c in range(4)]
        sums_sb = [E(nc.sbuf_tensor(f"sums_sb{i}", [128, T], F32))
                   for i in range(2)]
        max_sb = E(nc.sbuf_tensor("max_sb", [128, 16], F32))
        PS = E(nc.psum_tensor("PS", [128, 4096], F32))

        block = E(nc.Block())
        ctx = Ctx(nc, base)
        nc.const_aps.register(F32, SQ512, cst[:, 0:1])
        nc.const_aps.register(F32, 2.0, cst[:, 1:2])

        def BK(b, c0=0, c1=512, r0=0, r1=128):
            return PS[r0:r1, 512 * b + c0: 512 * b + c1]

        bank_tok = [None] * 8

        def xb_sl(buf, dc, c0, n, tiles=None):
            return buf[:, dc * XW + c0: dc * XW + c0 + n]

        def strided_tok(buf, dc, chw, j0, ntile, wid=128):
            return buf[:, dc * chw + j0 * 129: dc * chw + (j0 + ntile) * 129] \
                .rearrange("p (a b) -> p a b", b=129)[:, :, 0:wid]

        def setup():
            def v0(v):
                v.memset(eyef[:], 1.0)
                v.memset(onesf_row[:], 1.0)
                v.memset(onesb_col[:], 1.0)
                v.memset(indf[:], 1.0)
                v.memset(eps_col[:], EPS)
                v.memset(cst[:, 0:1], SQ512)
                v.memset(cst[:, 1:2], 2.0)
                v.memset(u0[:, 0:64], 1.0)
                v.memset(u0[:, 64:128], 0.0)
                v.memset(u1[:, 0:64], 0.0)
                v.memset(u1[:, 64:128], 1.0)
                ctx.inc("vS", v.memset(
                    x1g[:, 128: CH * X1W: 129], 1.0))
            block.vector(v0)

            def g0(g):
                ctx.wait(g, "vS")
                ctx.inc("gS", g.affine_select(
                    out=eyef[:], in_=eyef[:], pattern=[[1, 128]],
                    compare_op=ALU.is_equal, fill=0.0,
                    base=0, channel_multiplier=-1))
                ctx.wait(g, "gS")
                ctx.inc("gS", g.affine_select(
                    out=indf[:], in_=indf[:], pattern=[[1, ER]],
                    compare_op=ALU.is_ge, fill=0.0,
                    base=0, channel_multiplier=-RK))
                ctx.wait(g, "gS")
                ctx.inc("gS", g.affine_select(
                    out=indf[:], in_=indf[:], pattern=[[-1, ER]],
                    compare_op=ALU.is_ge, fill=0.0,
                    base=RK - 1, channel_multiplier=RK))
                for (dst, src) in [(qT_im, qtil), (kT_im, ktil),
                                   (vT_im, vtil), (woT_im, wotil)]:
                    ctx.inc("dGw", g.dma_start(out=dst, in_=src), 16)
                ctx.inc("dGw", g.dma_start(out=aextT[:], in_=atil), 16)
                ctx.inc("dGw", g.dma_start(out=bcatT[:], in_=btil), 16)
            block.gpsimd(g0)

            def v1(v):
                ctx.wait(v, "gS")
                ctx.inc("vS", v.tensor_copy(eyeb[:], eyef[:]))
            block.vector(v1)

            def bias_cols(src_row_ap, n, dst_col, dstoff):
                nchunk = n // 128
                def yf(sync):
                    ctx.wait(sync, "vS")
                    ctx.inc("dY", sync.dma_start(
                        out=brow[0:1, 0:min(n, 512)],
                        in_=src_row_ap), 16)
                block.sync(yf)

                def tf(t):
                    ctx.wait(t, "dY")
                    ctx.wait(t, "vS")
                    ctx.wait(t, "gS")
                    for c in range(nchunk):
                        ctx.inc("tS", t.matmul(
                            PS[0:128, 5 * 512 + c: 5 * 512 + c + 1],
                            brow[0:1, c * 128:(c + 1) * 128], eyef[0:1, 0:1],
                            start=(c == 0), stop=(c == nchunk - 1),
                            is_transpose=True))
                block.tensor(tf)

                def vf(v):
                    ctx.wait(v, "tS")
                    ctx.inc("vS", v.tensor_copy(
                        dst_col[:, dstoff:dstoff + nchunk],
                        PS[0:128, 5 * 512: 5 * 512 + nchunk]))
                block.vector(vf)

            bias_cols(qb[None, :], FQ, qb_col, 0)
            bias_cols(kb[None, :], FQ, kb_col, 0)
            bias_cols(vb[None, :], FQ, vb_col, 0)
            for half in range(8):
                bias_cols(wob[None, half * 512:(half + 1) * 512], 512,
                          wob_col, half * 4)

            def t1(t):
                ctx.wait(t, "dGw")
                for dc in range(CH):
                    mm = t.matmul(BK(6, 0, MW, r1=1),
                                  onesb_col[:], aextT[:, dc * MW:(dc + 1) * MW],
                                  start=(dc == 0), stop=(dc == CH - 1))
                ctx.inc("tS", mm)
            block.tensor(t1)

            def v2(v):
                ctx.wait(v, "tS")
                ctx.inc("vS", v.tensor_copy(wsA[:], BK(6, 0, MW, r1=1)))
            block.vector(v2)
        setup()
        snap0 = {n: sv[1] for n, sv in ctx.sems.items()}
        for e in ("tensor", "vector", "scalar", "gpsimd", "sync"):
            def bf0(eng, snap0=snap0):
                for n, v in snap0.items():
                    if v > 0:
                        eng.wait_ge(ctx.sems[n][0], v)
            getattr(block, e)(bf0)

        def ln_gram(src_of, g0, pre_wait=None):
            toks = {}
            def tf(t):
                if pre_wait is not None:
                    pre_wait(t)
                for i in range(2):
                    ctx.wait_tok(t, bank_tok[i])
                    for dc in range(CH):
                        mm = t.matmul(BK(i, 0, 129),
                                      src_of(dc, g0 + i, 128),
                                      src_of(dc, g0 + i, 129),
                                      start=(dc == 0), stop=(dc == CH - 1))
                    ctx.inc("tS", mm)
                    toks[i] = ctx.val("tS")
            block.tensor(tf)
            return toks

        def ln_statsV(g0, gtoks):
            def vf(v):
                for i in range(2):
                    ti = g0 + i
                    ctx.wait(v, "tS", gtoks[i])
                    v.tensor_mul(trash[:], BK(i, 0, 128), eyef[:])
                    v.drain()
                    v.reduce_sum(ss_col[:, ti:ti + 1], trash[:], axis=AX.X)
                    v.drain()
                    ctx.inc("vS", v.tensor_copy(sx_col[:, ti:ti + 1],
                                                BK(i, 128, 129)))
                    bank_tok[i] = ctx.tok("vS")
                sl = slice(g0, g0 + 2)
                v.drain()
                v.tensor_scalar_mul(m_col[:, sl], sx_col[:, sl], -1.0 / D)
                v.tensor_scalar_mul(va_col[:, sl], ss_col[:, sl], 1.0 / D)
                v.drain()
                v.tensor_mul(vb2_col[:, sl], m_col[:, sl], m_col[:, sl])
                v.drain()
                ctx.inc("vS", v.tensor_sub(va_col[:, sl], va_col[:, sl],
                                           vb2_col[:, sl]))
            block.vector(vf)
            return ctx.val("vS")

        def ln_lnexp(g0, vtok):
            def sf(s):
                ctx.wait(s, "vS", vtok)
                s.activation(out=vb2_col[:, g0:g0 + 2], in_=va_col[:, g0:g0 + 2],
                             func=AF.Ln, bias=eps_col[:, 0:1])
                s.drain()
                ctx.inc("sS", s.activation(out=r_col[:, g0:g0 + 2],
                                           in_=vb2_col[:, g0:g0 + 2],
                                           func=AF.Exp, scale=-0.5))
            block.scalar(sf)
            return ctx.val("sS")

        def ln_rowsT(g0, stok):
            b2 = bank_tok[2]
            b3 = bank_tok[3]
            def tf(t):
                ctx.wait(t, "sS", stok)
                ctx.wait_tok(t, b2)
                ctx.wait_tok(t, b3)
                for i in range(2):
                    t.matmul(PS[0:1, 2 * 512 + i * 128: 2 * 512 + (i + 1) * 128],
                             m_col[:, g0 + i:g0 + i + 1], eyef[:],
                             start=(i == 0), stop=(i == 1), is_transpose=True)
                for i in range(2):
                    mm = t.matmul(
                        PS[0:1, 3 * 512 + i * 128: 3 * 512 + (i + 1) * 128],
                        r_col[:, g0 + i:g0 + i + 1], eyef[:],
                        start=(i == 0), stop=(i == 1), is_transpose=True)
                ctx.inc("tS", mm)
            block.tensor(tf)
            return ctx.val("tS")

        def ln_rowsV(g0, rtok):
            def vf(v):
                ctx.wait(v, "tS", rtok)
                t0 = g0 * 128
                v.tensor_copy(mrow[0:1, t0:t0 + 256], PS[0:1, 2 * 512:2 * 512 + 256])
                ctx.inc("vS", v.tensor_copy(rrow[0:1, t0:t0 + 256],
                                            PS[0:1, 3 * 512:3 * 512 + 256]))
            block.vector(vf)
            bank_tok[2] = ctx.tok("vS")
            bank_tok[3] = ctx.tok("vS")
            return ctx.val("vS")

        def mid_main(tb, src_mm):
            def tf(t):
                ctx.wait_tok(t, bank_tok[5])
                for dc in range(CH):
                    t.matmul(BK(5, 0, TB, r1=MW),
                             aextT[:, dc * MW:(dc + 1) * MW],
                             src_mm(dc), start=(dc == 0), stop=False)
            block.tensor(tf)

        def ffn_gate_delta(tb, src_res, evac_s, evac_v, v_mrow):
            t0 = tb * TB
            g0 = tb * 2
            sv = ctx.val("sS")
            vv = ctx.val("vS")
            b2 = bank_tok[2]
            b4 = [bank_tok[4 + k] for k in range(4)]
            b5 = bank_tok[5]
            ttoks = {}

            def tf(t):
                ctx.wait(t, "vS", v_mrow)
                mm = t.matmul(BK(5, 0, TB, r1=MW), wsA[:],
                              mrow[0:1, t0:t0 + TB], start=False, stop=True)
                ctx.inc("tS", mm)
                ttoks["aug"] = ctx.val("tS")
                ctx.wait_tok(t, b2)
                ctx.wait(t, "vS", vv + 1)
                for i in range(2):
                    mm = t.matmul(
                        PS[0:128, 2 * 512 + i * 4: 2 * 512 + (i + 1) * 4],
                        lrow[0:NE, i * 128:(i + 1) * 128], eyef[0:NE, 0:NE],
                        start=(i == 0), stop=(i == 1), is_transpose=True)
                ctx.inc("tS", mm)
                ttoks["lT"] = ctx.val("tS")
                ctx.wait(t, "vS", vv + 3)
                for i in range(2):
                    mm = t.matmul(
                        PS[0:NE, 2 * 512 + 128 + i * 128:
                           2 * 512 + 128 + (i + 1) * 128],
                        gsc[:, 52 + i * 4:52 + (i + 1) * 4], eyef[:],
                        start=(i == 0), stop=(i == 1), is_transpose=True)
                ctx.inc("tS", mm)
                ttoks["wT"] = ctx.val("tS")
                ctx.wait(t, "vS", vv + 4)
                ctx.wait(t, "sS", sv + 1)
                mm = t.matmul(BK(5, 0, TB, r1=ER), indf[:], wrow[:],
                              start=True, stop=True)
                ctx.inc("tS", mm)
                ttoks["ind"] = ctx.val("tS")
                ctx.wait(t, "vS", vv + 5)
                for oc in range(CH):
                    bkk = 4 + oc % 4
                    if oc >= 4:
                        prev = oc - 4
                        if prev % 2 == 0:
                            ctx.wait(t, "sS", sv + 2 + prev // 2 + 1)
                        else:
                            ctx.wait(t, "vS", vv + 5 + prev // 2 + 1)
                    else:
                        ctx.wait_tok(t, b4[oc])
                    t.matmul(BK(bkk, 0, TB), bcatT[:, oc * 128:(oc + 1) * 128],
                             gm_sb[:], start=True, stop=False)
                    mm = t.matmul(BK(bkk, 0, TB), eyeb[:], src_res(oc),
                                  start=False, stop=True)
                    ctx.inc("tS", mm)
                    ttoks[("d", oc)] = ctx.val("tS")
            block.tensor(tf)

            def sf(s):
                ctx.wait(s, "tS", ttoks["aug"])
                ctx.inc("sS", s.copy(mid_sb[:], BK(5, 0, TB, r1=ER)))
                ctx.wait(s, "vS", vv + 2)
                ctx.inc("sS", s.activation(out=gsc[:, 24:32], in_=ltok[:],
                                           func=AF.Exp))
                for oc in range(0, CH, 2):
                    ctx.wait(s, "tS", ttoks[("d", oc)])
                    evac_s(s, oc, 4 + oc % 4)
            block.scalar(sf)
            assert ctx.val("sS") == sv + 2 + 16, (ctx.val("sS"), sv + 18)

            def vf(v):
                ctx.wait(v, "tS", ttoks["aug"])
                ctx.inc("vS", v.tensor_copy(lrow[:], PS[ER:MW, 5 * 512:5 * 512 + TB]))
                ctx.wait(v, "tS", ttoks["lT"])
                v.tensor_copy(ltok[:], PS[0:128, 2 * 512:2 * 512 + 8])
                v.drain()
                for i in range(2):
                    v.tensor_single_scalar(ltok[:, i * 4:(i + 1) * 4],
                                           ltok[:, i * 4:(i + 1) * 4],
                                           r_col[:, g0 + i:g0 + i + 1],
                                           ALU.mult)
                v.drain()
                l_ = [ltok[:, e: 2 * NE: NE] for e in range(NE)]
                v.tensor_max(gsc[:, 0:2], l_[0], l_[1])
                v.tensor_tensor(gsc[:, 2:4], l_[0], l_[1], ALU.min)
                v.tensor_max(gsc[:, 4:6], l_[2], l_[3])
                v.tensor_tensor(gsc[:, 6:8], l_[2], l_[3], ALU.min)
                v.drain()
                v.tensor_tensor(gsc[:, 8:10], gsc[:, 0:2], gsc[:, 4:6], ALU.min)
                v.tensor_max(gsc[:, 10:12], gsc[:, 2:4], gsc[:, 6:8])
                v.drain()
                v.tensor_max(gsc[:, 12:14], gsc[:, 8:10], gsc[:, 10:12])
                v.drain()
                for e in range(NE):
                    v.tensor_tensor(gsc[:, 16 + e: 24: NE], l_[e],
                                    gsc[:, 12:14], ALU.is_ge)
                ctx.inc("vS", v.drain())
                ctx.wait(v, "sS", sv + 2)
                v.tensor_mul(gsc[:, 32:40], gsc[:, 24:32], gsc[:, 16:24])
                v.drain()
                v.tensor_add(gsc[:, 40:42], gsc[:, 32: 40: NE], gsc[:, 33: 40: NE])
                v.drain()
                v.tensor_add(gsc[:, 42:44], gsc[:, 34: 40: NE], gsc[:, 35: 40: NE])
                v.drain()
                v.tensor_add(gsc[:, 40:42], gsc[:, 40:42], gsc[:, 42:44])
                v.drain()
                v.tensor_scalar_mul(gsc[:, 44:46], gsc[:, 40:42], 0.5)
                v.drain()
                v.reciprocal(gsc[:, 46:48], gsc[:, 44:46])
                v.drain()
                for i in range(2):
                    v.tensor_mul(gsc[:, 48 + i:49 + i], gsc[:, 46 + i:47 + i],
                                 r_col[:, g0 + i:g0 + i + 1])
                v.drain()
                for e in range(NE):
                    v.tensor_mul(gsc[:, 52 + e: 60: NE], gsc[:, 32 + e: 40: NE],
                                 gsc[:, 48:50])
                ctx.inc("vS", v.drain())
                ctx.wait(v, "tS", ttoks["wT"])
                ctx.inc("vS", v.tensor_copy(
                    wrow[:], PS[0:NE, 2 * 512 + 128: 2 * 512 + 128 + TB]))
                ctx.wait(v, "tS", ttoks["ind"])
                ctx.inc("vS", v.tensor_mul(gm_sb[:], mid_sb[:],
                                           BK(5, 0, TB, r1=ER)))
                for oc in range(1, CH, 2):
                    ctx.wait(v, "tS", ttoks[("d", oc)])
                    evac_v(v, oc, 4 + oc % 4)
            block.vector(vf)
            assert ctx.val("vS") == vv + 5 + 16, (ctx.val("vS"), vv + 21)

            bank_tok[2] = ("vS", vv + 4)
            bank_tok[5] = ("vS", vv + 5)
            for oc in range(CH - 4, CH):
                bkk = 4 + oc % 4
                if oc % 2 == 0:
                    bank_tok[bkk] = ("sS", sv + 2 + oc // 2 + 1)
                else:
                    bank_tok[bkk] = ("vS", vv + 5 + oc // 2 + 1)
            return ttoks

        p1 = ExitStack()
        xbuf = [p1.enter_context(nc.sbuf_tensor(f"xb{i}", [128, CH * XW], BF16))
                for i in range(2)]
        wt = [p1.enter_context(nc.sbuf_tensor(f"wt{i}", [128, D], BF16))
              for i in range(2)]
        vt_g = p1.enter_context(nc.sbuf_tensor("vt_g", [128, 4 * PG], BF16))
        qkst = [p1.enter_context(nc.sbuf_tensor(f"qkst{i}", [128, PG], BF16))
                for i in range(2)]

        xb_cons = [None, None]
        wt_cons = [None, None]
        qkst_cons = [None, None]

        def v_ones_init(v):
            for b in range(2):
                v.memset(xbuf[b][:, 128: CH * XW: 129], 1.0)
            ctx.inc("vS", v.drain())
        block.vector(v_ones_init)

        def emit_xload(tb):
            if tb >= 8:
                return
            b = tb % 2
            def gf(g, tb=tb, b=b):
                ctx.wait_tok(g, xb_cons[b])
                for a in range(2):
                    dst = xbuf[b].rearrange("p (dc w) -> p dc w", w=XW)[
                        :, :, a * 129: a * 129 + 128]
                    src = xT[:, tb * TB + a * 128: tb * TB + (a + 1) * 128] \
                        .rearrange("(dc p) t -> p dc t", p=128)
                    ctx.inc("dGx%d" % b, g.dma_start(out=dst, in_=src), 16)
            block.gpsimd(gf)

        x_ld = {}
        emit_xload(0)
        x_ld[0] = ctx.tok("dGx0")
        emit_xload(1)
        x_ld[1] = ctx.tok("dGx1")

        wim = [qT_im] * 4 + [kT_im] * 4 + [vT_im] * 4
        wt_ld = {}

        def emit_wtload(pg, fi):
            if fi >= 12:
                return
            b = fi % 2
            c = fi % 4
            def sf(s, pg=pg, fi=fi, b=b, c=c):
                ctx.wait_tok(s, wt_cons[b])
                ctx.inc("dT%d" % b, s.dma_start(out=wt[b][:], in_=wim[fi][c]), 16)
            block.scalar(sf)
            wt_ld[(pg, fi)] = ctx.tok("dT%d" % b)

        def p1_evacs(tb, j):
            def evac_s(s, oc, bkk, j=j):
                dst = x1g[:, oc * X1W + j * 258: oc * X1W + (j + 1) * 258] \
                    .rearrange("p (a b) -> p a b", b=129)[:, :, 0:128]
                ctx.wait(s, "dY")
                ctx.inc("sS", s.copy(dst, BK(bkk, 0, TB)
                                     .rearrange("p (a b) -> p a b", b=128)))
            def evac_v(v, oc, bkk, j=j):
                dst = x1g[:, oc * X1W + j * 258: oc * X1W + (j + 1) * 258] \
                    .rearrange("p (a b) -> p a b", b=129)[:, :, 0:128]
                ctx.wait(v, "dY")
                ctx.inc("vS", v.tensor_copy(dst, BK(bkk, 0, TB)
                                            .rearrange("p (a b) -> p a b", b=128)))
            return evac_s, evac_v

        def emit_gram2_finish(prev):
            vtok2 = ln_statsV(prev["g0"], prev["g2toks"])
            stok2 = ln_lnexp(prev["g0"], vtok2)
            rtok2 = ln_rowsT(prev["g0"], stok2)
            ln_rowsV(prev["g0"], rtok2)
            def yf(sync, tb=prev["tb"], j=prev["j"]):
                ctx.wait(sync, "sS")
                ctx.wait(sync, "vS")
                for a2 in range(2):
                    a = j * 2 + a2
                    src = x1g.rearrange("p (dc w) -> p dc w", w=X1W)[
                        :, :, a * 129: a * 129 + 128]
                    dst = x1_dram[:, tb * TB + a2 * 128:
                                  tb * TB + (a2 + 1) * 128].rearrange(
                        "(dc p) t -> p dc t", p=128)
                    ctx.inc("dY", sync.dma_start(out=dst, in_=src), 16)
            block.sync(yf)

        def mk_srcs(tb):
            xb = xbuf[tb % 2]
            def src_x(dc, ti, n, xb=xb, tb=tb):
                lo = dc * XW + (ti - tb * 2) * 129
                return xb[:, lo: lo + n]
            def src_mm(dc, xb=xb):
                return strided_tok(xb, dc, XW, 0, 2, 128)
            def src_res(oc, xb=xb):
                return strided_tok(xb, oc, XW, 0, 2, 128)
            def pre(t, tb=tb):
                ctx.wait_tok(t, x_ld[tb])
                if tb == 0:
                    ctx.wait(t, "vS")
            return src_x, src_mm, src_res, pre

        prev = None
        nxt = None
        for pg in range(2):
            for j in range(4):
                tb = pg * 4 + j
                src_x, src_mm, src_res, pre = mk_srcs(tb)
                if nxt is None:
                    g1 = ln_gram(src_x, tb * 2, pre_wait=pre)
                    mid_main(tb, src_mm)
                else:
                    g1 = nxt["g1"]
                    if not nxt["mid"]:
                        mid_main(tb, src_mm)
                vtok1 = ln_statsV(tb * 2, g1)
                stok1 = ln_lnexp(tb * 2, vtok1)
                if prev is not None:
                    def pre2(t, prev=prev):
                        ctx.wait_tok(t, prev["etok_s"])
                        ctx.wait_tok(t, prev["etok_v"])
                    g2 = ln_gram(prev["srcg2"], prev["g0"], pre_wait=pre2)
                    vtok2 = ln_statsV(prev["g0"], g2)
                    stok2 = ln_lnexp(prev["g0"], vtok2)
                rtok1 = ln_rowsT(tb * 2, stok1)
                v_mrow = ln_rowsV(tb * 2, rtok1)
                g1n = None
                if tb + 1 < 8:
                    src_xn, src_mmn, _, pren = mk_srcs(tb + 1)
                    g1n = ln_gram(src_xn, (tb + 1) * 2, pre_wait=pren)
                if prev is not None:
                    rtok2 = ln_rowsT(prev["g0"], stok2)
                    ln_rowsV(prev["g0"], rtok2)
                    def yf(sync, tbp=prev["tb"], jp=prev["j"]):
                        ctx.wait(sync, "sS")
                        ctx.wait(sync, "vS")
                        for a2 in range(2):
                            aa = jp * 2 + a2
                            src = x1g.rearrange("p (dc w) -> p dc w", w=X1W)[
                                :, :, aa * 129: aa * 129 + 128]
                            dst = x1_dram[:, tbp * TB + a2 * 128:
                                          tbp * TB + (a2 + 1) * 128].rearrange(
                                "(dc p) t -> p dc t", p=128)
                            ctx.inc("dY", sync.dma_start(out=dst, in_=src), 16)
                    block.sync(yf)
                ev_s, ev_v = p1_evacs(tb, j)
                ffn_gate_delta(tb, src_res, ev_s, ev_v, v_mrow)
                xb_cons[tb % 2] = ctx.tok("tS")
                mid_ahead = (tb + 1 < 8) and (j != 3)
                if mid_ahead:
                    mid_main(tb + 1, src_mmn)
                emit_xload(tb + 2)
                if tb + 2 < 8:
                    x_ld[tb + 2] = ctx.tok("dGx%d" % (tb % 2))

                def srcg2(dc, ti, n, pg=pg):
                    lo = dc * X1W + (ti - pg * 8) * 129
                    return x1g[:, lo: lo + n]
                prev = {"tb": tb, "j": j, "g0": tb * 2, "srcg2": srcg2,
                        "etok_s": ctx.tok("sS"), "etok_v": ctx.tok("vS")}
                nxt = ({"g1": g1n, "mid": mid_ahead}
                       if g1n is not None else None)
            def pre3(t, prev=prev):
                ctx.wait_tok(t, prev["etok_s"])
                ctx.wait_tok(t, prev["etok_v"])
            g2 = ln_gram(prev["srcg2"], prev["g0"], pre_wait=pre3)
            vtok2 = ln_statsV(prev["g0"], g2)
            stok2 = ln_lnexp(prev["g0"], vtok2)
            rtok2 = ln_rowsT(prev["g0"], stok2)
            ln_rowsV(prev["g0"], rtok2)
            def yft(sync, tbp=prev["tb"], jp=prev["j"]):
                ctx.wait(sync, "sS")
                ctx.wait(sync, "vS")
                for a2 in range(2):
                    aa = jp * 2 + a2
                    src = x1g.rearrange("p (dc w) -> p dc w", w=X1W)[
                        :, :, aa * 129: aa * 129 + 128]
                    dst = x1_dram[:, tbp * TB + a2 * 128:
                                  tbp * TB + (a2 + 1) * 128].rearrange(
                        "(dc p) t -> p dc t", p=128)
                    ctx.inc("dY", sync.dma_start(out=dst, in_=src), 16)
            block.sync(yft)
            prev = None

            def trx(t, pg=pg):
                ctx.wait(t, "vS")
                ctx.wait_tok(t, bank_tok[6])
                ctx.wait_tok(t, bank_tok[7])
                for jb in range(2):
                    ctx.inc("tS", t.matmul(
                        BK(6 + jb), onesf_row[:],
                        rrow[0:1, pg * PG + jb * 512: pg * PG + (jb + 1) * 512],
                        start=True, stop=True))
            block.tensor(trx)

            def vrx(v):
                ctx.wait(v, "tS")
                for jb in range(2):
                    last = v.tensor_copy(rexp[:, jb * 512:(jb + 1) * 512],
                                         BK(6 + jb))
                ctx.inc("vS", last)
            block.vector(vrx)
            bank_tok[6] = ctx.tok("vS")
            bank_tok[7] = ctx.tok("vS")

            emit_wtload(pg, 0)
            emit_wtload(pg, 1)
            for fi in range(12):
                kind = fi // 4
                c = fi % 4
                wtb = wt[fi % 2]
                pair = (4, 5) if fi % 2 == 0 else (6, 7)

                if pg == 0:
                    def tws(t, fi=fi, wtb=wtb):
                        ctx.wait_tok(t, wt_ld[(0, fi)])
                        ctx.wait_tok(t, bank_tok[2])
                        for dc in range(CH):
                            mm = t.matmul(BK(2, 0, 128, r1=1), onesb_col[:],
                                          wtb[:, dc * 128:(dc + 1) * 128],
                                          start=(dc == 0), stop=(dc == CH - 1))
                        ctx.inc("tS", mm)
                    block.tensor(tws)

                    def vws(v, fi=fi):
                        ctx.wait(v, "tS")
                        ctx.inc("vS", v.tensor_copy(
                            wsQKV[0:1, fi * 128:(fi + 1) * 128],
                            BK(2, 0, 128, r1=1)))
                    block.vector(vws)
                    bank_tok[2] = ctx.tok("vS")

                def tp(t, fi=fi, pg=pg, wtb=wtb, pair=pair):
                    ctx.wait_tok(t, wt_ld[(pg, fi)])
                    ctx.wait(t, "vS")
                    ctx.wait_tok(t, bank_tok[pair[0]])
                    ctx.wait_tok(t, bank_tok[pair[1]])
                    for dc in range(CH):
                        for jb in range(2):
                            t.matmul(BK(pair[jb]),
                                     wtb[:, dc * 128:(dc + 1) * 128],
                                     strided_tok(x1g, dc, X1W, jb * 4, 4, 128),
                                     start=(dc == 0), stop=False)
                    for jb in range(2):
                        ctx.inc("tS", t.matmul(
                            BK(pair[jb]), wsQKV[0:1, fi * 128:(fi + 1) * 128],
                            mrow[0:1, pg * PG + jb * 512: pg * PG + (jb + 1) * 512],
                            start=False, stop=True))
                block.tensor(tp)
                wt_cons[fi % 2] = ctx.tok("tS")

                emit_wtload(pg, fi + 2)

                def vp(v, fi=fi, pair=pair):
                    ctx.wait(v, "tS")
                    ctx.wait(v, "sS")
                    ctx.wait_tok(v, qkst_cons[fi % 2])
                    v.tensor_mul(tmp_f[:, 0:512], BK(pair[0]), rexp[:, 0:512])
                    v.drain()
                    ctx.inc("vS", v.tensor_mul(tmp_f[:, 512:1024], BK(pair[1]),
                                               rexp[:, 512:1024]))
                block.vector(vp)
                bank_tok[pair[0]] = ctx.tok("vS")
                bank_tok[pair[1]] = ctx.tok("vS")

                bcol = [qb_col, kb_col, vb_col][kind]
                def sp(s, fi=fi, c=c, kind=kind, bcol=bcol):
                    ctx.wait(s, "vS")
                    if kind == 2:
                        dstap = vt_g[:, c * PG:(c + 1) * PG]
                    else:
                        dstap = qkst[fi % 2][:]
                    ctx.inc("sS", s.activation(out=dstap, in_=tmp_f[:, 0:1024],
                                               func=AF.Identity,
                                               bias=bcol[:, c:c + 1]))
                block.scalar(sp)

                if kind != 2:
                    def yq(sync, fi=fi, c=c, kind=kind, pg=pg):
                        ctx.wait(sync, "sS")
                        dstd = (q_dram if kind == 0 else k_dram)
                        ctx.inc("dQ%d" % (fi % 2), sync.dma_start(
                            out=dstd[c * 128:(c + 1) * 128,
                                     pg * PG:(pg + 1) * PG],
                            in_=qkst[fi % 2][:]), 16)
                    block.sync(yq)
                    qkst_cons[fi % 2] = ctx.tok("dQ%d" % (fi % 2))

            def yv(sync, pg=pg):
                ctx.wait(sync, "sS")
                for fc in range(4):
                    ctx.inc("dY", sync.dma_start(
                        out=v_dram[fc * 128:(fc + 1) * 128,
                                   pg * PG:(pg + 1) * PG],
                        in_=vt_g[:, fc * PG:(fc + 1) * PG]), 16)
            block.sync(yv)

        if "x1" in dbg_t:
            def ydbg(sync):
                ctx.wait(sync, "sS")
                ctx.wait(sync, "vS")
                ctx.wait(sync, "dY")
                src = x1_dram
                ctx.inc("dY", sync.dma_start(out=dbg_t["x1"], in_=src), 16)
            block.sync(ydbg)
        for nm, dr in (("q", q_dram), ("k", k_dram), ("v", v_dram)):
            if nm in dbg_t:
                def ydbg2(sync, nm=nm, dr=dr):
                    ctx.wait(sync, "dY")
                    ctx.wait(sync, "dQ0")
                    ctx.wait(sync, "dQ1")
                    ctx.inc("dY", sync.dma_start(out=dbg_t[nm], in_=dr), 16)
                block.sync(ydbg2)
        if "mrow" in dbg_t:
            def ydbg3(sync):
                ctx.wait(sync, "vS")
                ctx.inc("dY", sync.dma_start(out=dbg_t["mrow"], in_=mrow[:]), 16)
            block.sync(ydbg3)
        if "rrow" in dbg_t:
            def ydbg4(sync):
                ctx.wait(sync, "vS")
                ctx.inc("dY", sync.dma_start(out=dbg_t["rrow"], in_=rrow[:]), 16)
            block.sync(ydbg4)

        snap = {n: s[1] for n, s in ctx.sems.items()}
        for e in ("tensor", "vector", "scalar", "gpsimd", "sync"):
            def bf(eng, snap=snap):
                for n, v in snap.items():
                    if v > 0:
                        eng.wait_ge(ctx.sems[n][0], v)
            getattr(block, e)(bf)
        p1.close()

        p2 = ExitStack()
        v_sb = p2.enter_context(nc.sbuf_tensor("v_sb", [128, 16 * 640], BF16))
        Qaug = p2.enter_context(nc.sbuf_tensor("Qaug", [65, T], BF16))
        Kaug = p2.enter_context(nc.sbuf_tensor("Kaug", [65, T], BF16))
        probs = [p2.enter_context(nc.sbuf_tensor(f"probs{i}", [128, PG], BF16))
                 for i in range(2)]
        vstage = [p2.enter_context(nc.sbuf_tensor(f"vstage{i}", [128, FQ], BF16))
                  for i in range(2)]
        rstg = [p2.enter_context(nc.sbuf_tensor(f"rstg{i}", [1, T], BF16))
                for i in range(2)]
        mx2 = p2.enter_context(nc.sbuf_tensor("mx2", [128, 2], F32))

        def gx1(g):
            for a in range(8):
                dst = x1g.rearrange("p (dc w) -> p dc w", w=X1W)[
                    :, :, a * 129: a * 129 + 128]
                src = x1_dram[:, a * 128:(a + 1) * 128].rearrange(
                    "(dc p) t -> p dc t", p=128)
                ctx.inc("dGx0", g.dma_start(out=dst, in_=src), 16)
        block.gpsimd(gx1)
        x1_ld_pg0 = ctx.tok("dGx0")

        def p2v0(v):
            for kt in range(16):
                for h in range(NH):
                    v.memset(v_sb[:, kt * 640 + h * 80 + 64:
                                  kt * 640 + h * 80 + 65], 1.0)
            v.memset(Kaug[64:65, :], -1.0)
            ctx.inc("vS", v.drain())
        block.vector(p2v0)


        QA = [Qaug, p2.enter_context(nc.sbuf_tensor("Qaug1", [65, T], BF16))]
        KA = [Kaug, p2.enter_context(nc.sbuf_tensor("Kaug1", [65, T], BF16))]

        def p2v0b(v):
            ctx.inc("vS", v.memset(KA[1][64:65, :], -1.0))
        block.vector(p2v0b)

        ttok_prev = {}
        qk_ld = [None, None]
        for i in range(NH + 1):
            hA = i if i < NH else None
            hB = i - 1 if i >= 1 else None
            bi = i % 2
            bj = (i - 1) % 2
            dy_pre = ctx.tok("dY")

            if hA is not None:
                def ya(sync, hA=hA, bi=bi, tprev=dict(ttok_prev)):
                    if ("B", 31) in tprev:
                        ctx.wait(sync, "tS", tprev[("B", 31)])
                    else:
                        ctx.wait(sync, "tS")
                    ctx.inc("dY", sync.dma_start(
                        out=QA[bi][0:64, :],
                        in_=q_dram[hA * 64:(hA + 1) * 64, :]), 16)
                    ctx.inc("dY", sync.dma_start(
                        out=KA[bi][0:64, :],
                        in_=k_dram[hA * 64:(hA + 1) * 64, :]), 16)
                block.sync(ya)
                qk_ld[bi] = ctx.tok("dY")

            vbase = ctx.val("vS")
            vtok = {}
            c = vbase
            if hA is not None:
                for q in range(8):
                    c += 1; vtok[("rm1", q)] = c
                    c += 1; vtok[("rm2", q)] = c
                    if i == 0:
                        c += 1; vtok[("cp", q)] = c
            if hB is not None:
                c += 1; vtok[("vo", 0)] = c
                c += 1; vtok[("so", 0)] = c
            if hA is not None:
                for q in range(8, 16):
                    c += 1; vtok[("rm1", q)] = c
                    c += 1; vtok[("rm2", q)] = c
                    if i == 0:
                        c += 1; vtok[("cp", q)] = c
                c += 1; vtok[("vc",)] = c
            if hB is not None:
                c += 1; vtok[("vo", 1)] = c
                c += 1; vtok[("so", 1)] = c
            vend = c

            dtr = {}
            if i == 0:
                def p2s0b(s, vtok=vtok, dtr=dtr):
                    for kt in range(16):
                        if kt < 2:
                            ctx.wait_tok(s, dy_pre)
                        else:
                            ctx.wait(s, "vS", vtok[("cp", kt - 2)])
                        ctx.inc("dT%d" % (kt % 2), s.dma_start(
                            out=vstage[kt % 2][:],
                            in_=v_dram[:, kt * 128:(kt + 1) * 128],
                            transpose=True), 16)
                        dtr[kt] = ctx.tok("dT%d" % (kt % 2))
                block.scalar(p2s0b)
            sbase = ctx.val("sS")
            stok = {j: sbase + j + 1 for j in range(32)} if hB is not None else {}
            send = sbase + (32 if hB is not None else 0)

            ttok = {}

            def tbody(t, hA=hA, hB=hB, bi=bi, bj=bj, vtok=vtok, stok=stok,
                      vbase=vbase, ttok=ttok, qk=dict(q0=qk_ld[bi])):
                for step in range(16):
                    if step == 0:
                        ctx.wait(t, "vS", vbase)
                    if hA is not None:
                        if step == 0:
                            ctx.wait_tok(t, qk["q0"])
                        else:
                            ctx.wait(t, "vS", vtok[("rm2", step - 1)])
                        for kb_ in range(2):
                            mm = t.matmul(BK(kb_),
                                          QA[bi][0:64, step * 128:(step + 1) * 128],
                                          KA[bi][0:64, kb_ * 512:(kb_ + 1) * 512],
                                          start=True, stop=True)
                        ctx.inc("tS", mm)
                        ttok[("A1", step)] = ctx.val("tS")
                    for half in range(2):
                        if hB is not None:
                            jj = 2 * step + half
                            qh, kt = jj // 16, jj % 16
                            pr = 4 + (jj % 2) * 2
                            if jj >= 2:
                                ctx.wait(t, "sS", stok[jj - 2])
                            if qh == 1 and kt == 0:
                                ctx.wait(t, "vS", vtok[("so", 0)])
                            for js in range(2):
                                mm = t.matmul(
                                    BK(pr + js),
                                    KA[bj][0:65, kt * 128:(kt + 1) * 128],
                                    QA[bj][0:65, qh * PG + js * 512:
                                           qh * PG + (js + 1) * 512],
                                    start=True, stop=True)
                            ctx.inc("tS", mm)
                            ttok[("B", jj)] = ctx.val("tS")
                            ctx.wait(t, "sS", stok[jj])
                            for js in range(2):
                                mm = t.matmul(
                                    PS[0:65, (2 + js) * 512:(2 + js) * 512 + 512],
                                    v_sb[:, kt * 640 + hB * 80:
                                         kt * 640 + hB * 80 + 65],
                                    probs[jj % 2][:, js * 512:(js + 1) * 512],
                                    start=(kt == 0), stop=(kt == 15))
                            ctx.inc("tS", mm)
                            ttok[("AV", jj)] = ctx.val("tS")
                        if hA is not None and half == 0:
                            ctx.wait(t, "vS", vtok[("rm1", step)])
                            for kb_ in range(2):
                                mm = t.matmul(
                                    BK(kb_),
                                    QA[bi][0:64, step * 128:(step + 1) * 128],
                                    KA[bi][0:64, 1024 + kb_ * 512:
                                           1024 + (kb_ + 1) * 512],
                                    start=True, stop=True)
                            ctx.inc("tS", mm)
                            ttok[("A2", step)] = ctx.val("tS")
                if hA is not None:
                    ctx.wait(t, "vS", vtok[("rm2", 15)])
                    if hB is not None:
                        ctx.wait(t, "sS", stok[31])
                    for qt in range(16):
                        bkk = 4 + qt // 4
                        mm = t.matmul(
                            PS[0:1, bkk * 512 + (qt % 4) * 128:
                               bkk * 512 + (qt % 4 + 1) * 128],
                            max_sb[:, qt:qt + 1], eyef[:],
                            start=(qt % 4 == 0), stop=(qt % 4 == 3),
                            is_transpose=True)
                        if qt % 4 == 3:
                            ctx.inc("tS", mm)
                    ttok[("cT",)] = ctx.val("tS")
            block.tensor(tbody)

            if hB is not None:
                def sbody(s, hB=hB, ttok=ttok, stok=stok):
                    for jj in range(32):
                        ctx.wait(s, "tS", ttok[("B", jj)])
                        if jj >= 2:
                            ctx.wait(s, "tS", ttok[("AV", jj - 2)])
                        pr = 4 + (jj % 2) * 2
                        ctx.inc("sS", s.activation(
                            out=probs[jj % 2][:],
                            in_=PS[0:128, pr * 512:pr * 512 + 1024],
                            func=AF.Exp))
                block.scalar(sbody)
                assert ctx.val("sS") == send

            def vbody(v, hA=hA, hB=hB, bi=bi, ttok=ttok, i=i, dtr=dtr):
                def rmq(q):
                    ctx.wait(v, "tS", ttok[("A1", q)])
                    v.reduce_max(mx2[:, 0:1], PS[0:128, 0:1024], axis=AX.X)
                    ctx.inc("vS", v.drain())
                    ctx.wait(v, "tS", ttok[("A2", q)])
                    v.reduce_max(mx2[:, 1:2], PS[0:128, 0:1024], axis=AX.X)
                    v.drain()
                    ctx.inc("vS", v.tensor_max(max_sb[:, q:q + 1],
                                               mx2[:, 0:1], mx2[:, 1:2]))
                    if i == 0:
                        ctx.wait_tok(v, dtr[q])
                        dst = v_sb[:, q * 640:(q + 1) * 640].rearrange(
                            "p (h e) -> p h e", h=NH)[:, :, 0:64]
                        src = vstage[q % 2].rearrange("p (h e) -> p h e", h=NH)
                        ctx.inc("vS", v.tensor_copy(dst, src))
                if hA is not None:
                    for q in range(8):
                        rmq(q)
                if hB is not None:
                    ctx.wait(v, "tS", ttok[("AV", 15)])
                    r0 = (hB % 2) * 64
                    ctx.inc("vS", v.tensor_copy(
                        attn[hB // 2][r0:r0 + 64, 0:PG],
                        PS[0:64, 2 * 512: 2 * 512 + 1024]))
                    r1 = (hB % 4) * 32
                    ctx.inc("vS", v.tensor_scalar_mul(
                        sums_sb[hB // 4][r1:r1 + 1, 0:PG],
                        PS[64:65, 2 * 512: 2 * 512 + 1024], SQ512))
                if hA is not None:
                    for q in range(8, 16):
                        rmq(q)
                    ctx.wait(v, "tS", ttok[("cT",)])
                    for g4 in range(4):
                        last = v.tensor_copy(
                            QA[bi][64:65, g4 * 512:(g4 + 1) * 512],
                            PS[0:1, (4 + g4) * 512:(4 + g4) * 512 + 512])
                    ctx.inc("vS", last)
                if hB is not None:
                    ctx.wait(v, "tS", ttok[("AV", 31)])
                    r0 = (hB % 2) * 64
                    ctx.inc("vS", v.tensor_copy(
                        attn[hB // 2][r0:r0 + 64, PG:2 * PG],
                        PS[0:64, 2 * 512: 2 * 512 + 1024]))
                    r1 = (hB % 4) * 32
                    ctx.inc("vS", v.tensor_scalar_mul(
                        sums_sb[hB // 4][r1:r1 + 1, PG:2 * PG],
                        PS[64:65, 2 * 512: 2 * 512 + 1024], SQ512))
            block.vector(vbody)
            assert ctx.val("vS") == vend, (ctx.val("vS"), vend)
            ttok_prev = ttok
            ttok_prev = ttok

        def vr(v):
            ctx.wait(v, "vS")
            ctx.wait(v, "sS")
            ctx.wait(v, "dY")
            ctx.inc("vS", v.drain())
        block.vector(vr)

        for c in range(4):
            def vstg(v, c=c):
                ctx.wait(v, "vS")
                h0, h1 = 2 * c, 2 * c + 1
                v.tensor_copy(
                    rstg[0][:],
                    sums_sb[h0 // 4][(h0 % 4) * 32:(h0 % 4) * 32 + 1, :])
                ctx.inc("vS", v.tensor_copy(
                    rstg[1][:],
                    sums_sb[h1 // 4][(h1 % 4) * 32:(h1 % 4) * 32 + 1, :]))
            block.vector(vstg)
            for jb in range(4):
                def te(t, c=c, jb=jb):
                    ctx.wait(t, "vS")
                    bkk = jb % 2
                    t.matmul(BK(bkk), u0[:],
                             rstg[0][0:1, jb * 512:(jb + 1) * 512],
                             start=True, stop=False)
                    ctx.inc("tS", t.matmul(
                        BK(bkk), u1[:],
                        rstg[1][0:1, jb * 512:(jb + 1) * 512],
                        start=False, stop=True))
                block.tensor(te)

                def ve(v, c=c, jb=jb):
                    ctx.wait(v, "tS")
                    bkk = jb % 2
                    v.reciprocal(tmp_f[:, 0:512], BK(bkk))
                    v.drain()
                    ctx.inc("vS", v.tensor_mul(
                        attn[c][:, jb * 512:(jb + 1) * 512],
                        attn[c][:, jb * 512:(jb + 1) * 512],
                        tmp_f[:, 0:512]))
                block.vector(ve)

        if "probs" in dbg_t:
            def ypr(sync):
                ctx.wait(sync, "sS")
                ctx.wait(sync, "tS")
                ctx.inc("dY", sync.dma_start(out=dbg_t["probs"],
                                             in_=probs[1][:]), 16)
            block.sync(ypr)
        if "qaug" in dbg_t:
            def yqa(sync):
                ctx.wait(sync, "vS")
                ctx.wait(sync, "tS")
                ctx.inc("dY", sync.dma_start(out=dbg_t["qaug"], in_=Qaug[:]), 16)
            block.sync(yqa)
        if "maxs" in dbg_t:
            def yma(sync):
                ctx.wait(sync, "vS")
                ctx.inc("dY", sync.dma_start(out=dbg_t["maxs"], in_=max_sb[:]), 16)
            block.sync(yma)
        if "vstg" in dbg_t:
            def yvg(sync):
                ctx.wait(sync, "vS")
                ctx.wait(sync, "dT0")
                ctx.inc("dY", sync.dma_start(out=dbg_t["vstg"],
                                             in_=vstage[1][:]), 16)
            block.sync(yvg)
        if "vsb" in dbg_t:
            def yvs(sync):
                ctx.wait(sync, "vS")
                ctx.wait(sync, "dT0")
                src = v_sb.rearrange("p (kt h e) -> p kt h e", kt=16, h=NH)[
                    :, :, :, 0:65]
                ctx.inc("dY", sync.dma_start(out=dbg_t["vsb"], in_=src), 16)
            block.sync(yvs)
        if "attn" in dbg_t:
            def yat(sync):
                ctx.wait(sync, "vS")
                for c in range(4):
                    ctx.inc("dY", sync.dma_start(
                        out=dbg_t["attn"][c * 128:(c + 1) * 128, :],
                        in_=attn[c][:]), 16)
            block.sync(yat)
        if "sums" in dbg_t:
            def ysm(sync):
                ctx.wait(sync, "sS")
                for h in range(NH):
                    ctx.inc("dY", sync.dma_start(
                        out=dbg_t["sums"][h:h + 1, :],
                        in_=sums_sb[h // 4][(h % 4) * 32:(h % 4) * 32 + 1, :]), 16)
            block.sync(ysm)

        snap = {n: s[1] for n, s in ctx.sems.items()}
        for e in ("tensor", "vector", "scalar", "gpsimd", "sync"):
            def bf2(eng, snap=snap):
                for n, v in snap.items():
                    if v > 0:
                        eng.wait_ge(ctx.sems[n][0], v)
            getattr(block, e)(bf2)
        p2.close()
        for b in range(8):
            bank_tok[b] = None

        p3 = ExitStack()
        wot = [p3.enter_context(nc.sbuf_tensor(f"wot{i}", [128, FQ], BF16))
               for i in range(2)]
        outst = [p3.enter_context(nc.sbuf_tensor(f"outst{i}", [128, TB], F32))
                 for i in range(4)]
        outst_cons = [None] * 4
        wot_cons = [None, None]
        wot_ld = {}

        def emit_wotload(pg, dm):
            if dm >= CH:
                return
            b = dm % 2
            def sf(s, pg=pg, dm=dm, b=b):
                ctx.wait_tok(s, wot_cons[b])
                ctx.inc("dT%d" % b, s.dma_start(out=wot[b][:], in_=woT_im[dm]), 16)
            block.scalar(sf)
            wot_ld[(pg, dm)] = ctx.tok("dT%d" % b)

        for pg in range(2):
            if pg == 0:
                x1_ld = x1_ld_pg0
            else:
                def yr(sync, pg=pg):
                    ctx.wait(sync, "tS")
                    ctx.wait(sync, "sS")
                    for a in range(8):
                        dst = x1g.rearrange("p (dc w) -> p dc w", w=X1W)[
                            :, :, a * 129: a * 129 + 128]
                        src = x1_dram[:, pg * PG + a * 128:
                                      pg * PG + (a + 1) * 128].rearrange(
                            "(dc p) t -> p dc t", p=128)
                        ctx.inc("dY", sync.dma_start(out=dst, in_=src), 16)
                block.sync(yr)
                x1_ld = ctx.tok("dY")

            emit_wotload(pg, 0)
            emit_wotload(pg, 1)
            for dm in range(CH):
                pair = (0, 1) if dm % 2 == 0 else (2, 3)
                def tw2(t, pg=pg, dm=dm, pair=pair, x1_ld=x1_ld):
                    ctx.wait_tok(t, wot_ld[(pg, dm)])
                    ctx.wait_tok(t, bank_tok[pair[0]])
                    ctx.wait_tok(t, bank_tok[pair[1]])
                    for fc in range(4):
                        for js in range(2):
                            t.matmul(BK(pair[js]),
                                     wot[dm % 2][:, fc * 128:(fc + 1) * 128],
                                     attn[fc][:, pg * PG + js * 512:
                                               pg * PG + (js + 1) * 512],
                                     start=(fc == 0), stop=False)
                    ctx.wait_tok(t, x1_ld)
                    for js in range(2):
                        ctx.inc("tS", t.matmul(
                            BK(pair[js]), eyeb[:],
                            strided_tok(x1g, dm, X1W, js * 4, 4, 128),
                            start=False, stop=True))
                block.tensor(tw2)
                wot_cons[dm % 2] = ctx.tok("tS")
                emit_wotload(pg, dm + 2)

                def sw2(s, dm=dm, pair=pair):
                    ctx.wait(s, "tS")
                    dst = x1g[:, dm * X1W: dm * X1W + 8 * 129].rearrange(
                        "p (a b) -> p a b", b=129)[:, :, 0:128]
                    ctx.inc("sS", s.activation(
                        out=dst, in_=PS[0:128, pair[0] * 512: pair[0] * 512 + 1024]
                        .rearrange("p (a b) -> p a b", b=128),
                        func=AF.Identity, bias=wob_col[:, dm:dm + 1]))
                block.scalar(sw2)
                bank_tok[pair[0]] = ctx.tok("sS")
                bank_tok[pair[1]] = ctx.tok("sS")

            if "x2" in dbg_t and pg == 0:
                pass

            for j in range(4):
                tb = pg * 4 + j

                def src_x2(dc, ti, n, pg=pg):
                    lo = dc * X1W + (ti - pg * 8) * 129
                    return x1g[:, lo: lo + n]

                def src_mm2(dc, j=j):
                    return strided_tok(x1g, dc, X1W, j * 2, 2, 128)

                def src_res2(oc, j=j):
                    return strided_tok(x1g, oc, X1W, j * 2, 2, 128)

                def pre(t):
                    ctx.wait(t, "sS")
                g3 = ln_gram(src_x2, tb * 2, pre_wait=pre if j == 0 else None)
                mid_main(tb, src_mm2)
                vtok = ln_statsV(tb * 2, g3)
                stok = ln_lnexp(tb * 2, vtok)
                rtok = ln_rowsT(tb * 2, stok)
                v_mrow = ln_rowsV(tb * 2, rtok)

                dO_base = [ctx.val("dO%d" % ob) for ob in range(4)]
                etoks = {}

                def evac_s(s, oc, bkk, dO_base=dO_base, etoks=etoks):
                    ob = oc % 4
                    want = dO_base[ob] + 16 * (oc // 4)
                    if want > 0:
                        ctx.wait(s, "dO%d" % ob, want)
                    ctx.inc("sS", s.copy(outst[ob][:], BK(bkk, 0, TB)))
                    etoks[oc] = ctx.tok("sS")

                def evac_v(v, oc, bkk, dO_base=dO_base, etoks=etoks):
                    ob = oc % 4
                    want = dO_base[ob] + 16 * (oc // 4)
                    if want > 0:
                        ctx.wait(v, "dO%d" % ob, want)
                    ctx.inc("vS", v.tensor_copy(outst[ob][:], BK(bkk, 0, TB)))
                    etoks[oc] = ctx.tok("vS")

                ffn_gate_delta(tb, src_res2, evac_s, evac_v, v_mrow)

                def yout(sync, tb=tb, etoks=etoks):
                    for oc in range(CH):
                        ctx.wait_tok(sync, etoks[oc])
                        ctx.inc("dO%d" % (oc % 4), sync.dma_start(
                            out=outT[oc * 128:(oc + 1) * 128,
                                     tb * TB:(tb + 1) * TB],
                            in_=outst[oc % 4][:]), 16)
                block.sync(yout)
        def yfin(sync):
            ctx.wait(sync, "dY")
            for ob in range(4):
                ctx.wait(sync, "dO%d" % ob)
        block.sync(yfin)
        p3.close()

    return nc


_NC_CACHE = {}


def _get_nc(dbg=()):
    key = tuple(sorted(dbg))
    if key not in _NC_CACHE:
        _NC_CACHE[key] = build_program(dbg=key)
    return _NC_CACHE[key]


def _tile_w(w):
    fo, d = w.shape
    return np.ascontiguousarray(
        w.reshape(fo // 128, 128, d // 128, 128).transpose(0, 3, 2, 1)
        .reshape(fo // 128, 128, d))


def make_inmaps(L0_attn_x, L0_attn_Wqk_w, L0_attn_Wqk_b, L0_attn_Wv_w,
                L0_attn_Wv_b, L0_attn_Wo_w, L0_attn_Wo_b, L0_ffn_router_w,
                L0_ffn_A, L0_ffn_B):
    x = np.asarray(L0_attn_x, np.float32)
    wqk = np.asarray(L0_attn_Wqk_w, np.float32)
    wqkb = np.asarray(L0_attn_Wqk_b, np.float32)
    aext = np.concatenate(
        [np.asarray(L0_ffn_A, np.float32).reshape(ER, D),
         np.asarray(L0_ffn_router_w, np.float32)], axis=0)
    atil = np.ascontiguousarray(
        aext.reshape(MW, CH, 128).transpose(2, 1, 0).reshape(128, CH * MW))
    btil = np.ascontiguousarray(
        np.asarray(L0_ffn_B, np.float32).transpose(0, 2, 1).reshape(ER, D))
    shared = {
        "qtil": _tile_w(wqk[0::2]),
        "ktil": _tile_w(wqk[1::2]),
        "vtil": _tile_w(np.asarray(L0_attn_Wv_w, np.float32)),
        "qb": np.ascontiguousarray(wqkb[0::2]),
        "kb": np.ascontiguousarray(wqkb[1::2]),
        "vb": np.ascontiguousarray(np.asarray(L0_attn_Wv_b, np.float32)),
        "wotil": _tile_w(np.asarray(L0_attn_Wo_w, np.float32)),
        "wob": np.ascontiguousarray(np.asarray(L0_attn_Wo_b, np.float32)),
        "atil": atil,
        "btil": btil,
    }
    return [dict(shared, xT=np.ascontiguousarray(x[b].T)) for b in range(8)]


def kernel(**inputs):
    in_maps = make_inmaps(**inputs)
    nc = _get_nc()
    res = run_bass_kernel_spmd(nc, in_maps, core_ids=list(range(8)))
    return np.stack([np.asarray(r["outT"]).T for r in res.results]).astype(np.float32)


# version 17
# speedup vs baseline: 1.0258x; 1.0095x over previous
import numpy as np
from contextlib import ExitStack

import concourse.bass as bass
import concourse.mybir as mybir
from concourse.bass_utils import run_bass_kernel_spmd

F32 = mybir.dt.float32
BF16 = mybir.dt.bfloat16
F32R = mybir.dt.float32r
AX = mybir.AxisListType
AF = mybir.ActivationFunctionType
ALU = mybir.AluOpType

D = 4096
T = 2048
CH = 32
NH = 8
HE = 64
FQ = 512
NE = 4
RK = 16
ER = 64
MW = 68
TB = 256
PG = 1024
EPS = 1e-5
SQ512 = float(np.sqrt(512.0))
XW = 258
X1W = 1032


class Ctx:

    def __init__(self, nc, st):
        self.nc = nc
        self.sems = {}
        for n in ["tS", "vS", "sS", "gS", "dY", "dGw", "dGx0", "dGx1",
                  "dT0", "dT1", "dQ0", "dQ1", "dO0", "dO1", "dO2", "dO3"]:
            self.sems[n] = [st.enter_context(nc.semaphore(n)), 0]

    def inc(self, name, instr, n=1, account=None):
        s = self.sems[name]
        instr.then_inc(s[0], n)
        s[1] += n if account is None else account
        return instr

    def tok(self, name):
        return (name, self.sems[name][1])

    def val(self, name):
        return self.sems[name][1]

    def wait(self, eng, name, val=None):
        s = self.sems[name]
        v = s[1] if val is None else val
        if v > 0:
            eng.wait_ge(s[0], v)

    def wait_tok(self, eng, tok):
        if tok is not None and tok[1] > 0:
            eng.wait_ge(self.sems[tok[0]][0], tok[1])


def build_program(dbg=()):
    nc = bass.Bass("TRN2", target_bir_lowering=False, debug=False)

    xT = nc.dram_tensor("xT", [D, T], F32, kind="ExternalInput").ap()
    qtil = nc.dram_tensor("qtil", [4, 128, D], F32, kind="ExternalInput").ap()
    ktil = nc.dram_tensor("ktil", [4, 128, D], F32, kind="ExternalInput").ap()
    vtil = nc.dram_tensor("vtil", [4, 128, D], F32, kind="ExternalInput").ap()
    qb = nc.dram_tensor("qb", [FQ], F32, kind="ExternalInput").ap()
    kb = nc.dram_tensor("kb", [FQ], F32, kind="ExternalInput").ap()
    vb = nc.dram_tensor("vb", [FQ], F32, kind="ExternalInput").ap()
    wotil = nc.dram_tensor("wotil", [CH, 128, FQ], F32, kind="ExternalInput").ap()
    wob = nc.dram_tensor("wob", [D], F32, kind="ExternalInput").ap()
    atil = nc.dram_tensor("atil", [128, CH * MW], F32, kind="ExternalInput").ap()
    btil = nc.dram_tensor("btil", [ER, D], F32, kind="ExternalInput").ap()
    outT = nc.dram_tensor("outT", [D, T], F32, kind="ExternalOutput").ap()

    qT_im = nc.dram_tensor("qT_im", [4, 128, D], BF16, kind="Internal").ap()
    kT_im = nc.dram_tensor("kT_im", [4, 128, D], BF16, kind="Internal").ap()
    vT_im = nc.dram_tensor("vT_im", [4, 128, D], BF16, kind="Internal").ap()
    woT_im = nc.dram_tensor("woT_im", [CH, 128, FQ], BF16, kind="Internal").ap()
    x1_dram = nc.dram_tensor("x1_dram", [D, T], BF16, kind="Internal").ap()
    v_dram = nc.dram_tensor("v_dram", [FQ, T], BF16, kind="Internal").ap()
    q_dram = nc.dram_tensor("q_dram", [FQ, T], BF16, kind="Internal").ap()
    k_dram = nc.dram_tensor("k_dram", [FQ, T], BF16, kind="Internal").ap()

    dbg_t = {}
    for name in dbg:
        shp = {"x1": [D, T], "q": [FQ, T], "k": [FQ, T], "v": [FQ, T],
               "attn": [FQ, T], "sums": [NH, T], "mrow": [1, T],
               "rrow": [1, T], "qaug": [65, T], "maxs": [128, 16],
               "vsb": [128, 16 * 520], "probs": [128, PG],
               "attnu": [FQ, T], "vstg": [128, FQ]}[name]
        dt = F32 if name in ("sums", "rrow", "maxs") else BF16
        dbg_t[name] = nc.dram_tensor("dbg_" + name, shp, dt,
                                     kind="ExternalOutput").ap()

    base = ExitStack()
    with base:
        E = base.enter_context

        eyef = E(nc.sbuf_tensor("eyef", [128, 128], F32))
        eyeb = E(nc.sbuf_tensor("eyeb", [128, 128], BF16))
        onesf_row = E(nc.sbuf_tensor("onesf_row", [1, 128], F32))
        onesb_col = E(nc.sbuf_tensor("onesb_col", [128, 1], BF16))
        indf = E(nc.sbuf_tensor("indf", [NE, ER], F32))
        u0 = E(nc.sbuf_tensor("u0", [1, 128], BF16))
        u1 = E(nc.sbuf_tensor("u1", [1, 128], BF16))
        eps_col = E(nc.sbuf_tensor("eps_col", [128, 1], F32))
        cst = E(nc.sbuf_tensor("cst", [128, 2], F32))
        qb_col = E(nc.sbuf_tensor("qb_col", [128, 4], F32))
        kb_col = E(nc.sbuf_tensor("kb_col", [128, 4], F32))
        vb_col = E(nc.sbuf_tensor("vb_col", [128, 4], F32))
        wob_col = E(nc.sbuf_tensor("wob_col", [128, 32], F32))
        brow = E(nc.sbuf_tensor("brow", [1, 512], F32))
        wsA = E(nc.sbuf_tensor("wsA", [1, MW], BF16))
        wsQKV = E(nc.sbuf_tensor("wsQKV", [1, 3 * FQ], BF16))
        sx_col = E(nc.sbuf_tensor("sx_col", [128, 16], F32))
        ss_col = E(nc.sbuf_tensor("ss_col", [128, 16], F32))
        m_col = E(nc.sbuf_tensor("m_col", [128, 16], F32))
        r_col = E(nc.sbuf_tensor("r_col", [128, 16], F32))
        va_col = E(nc.sbuf_tensor("va_col", [128, 16], F32))
        vb2_col = E(nc.sbuf_tensor("vb2_col", [128, 16], F32))
        mrow = E(nc.sbuf_tensor("mrow", [1, T], BF16))
        rrow = E(nc.sbuf_tensor("rrow", [1, T], F32))
        rexp = E(nc.sbuf_tensor("rexp", [128, PG], F32))
        trash = E(nc.sbuf_tensor("trash", [128, 128], F32))
        mid_sb = E(nc.sbuf_tensor("mid_sb", [ER, TB], BF16))
        gm_sb = E(nc.sbuf_tensor("gm_sb", [ER, TB], BF16))
        lrow = E(nc.sbuf_tensor("lrow", [NE, TB], F32))
        ltok = E(nc.sbuf_tensor("ltok", [128, 2 * NE], F32))
        gsc = E(nc.sbuf_tensor("gsc", [128, 64], F32))
        wrow = E(nc.sbuf_tensor("wrow", [NE, TB], F32))
        tmp_f = E(nc.sbuf_tensor("tmp_f", [128, PG], F32))
        aextT = E(nc.sbuf_tensor("aextT", [128, CH * MW], BF16))
        bcatT = E(nc.sbuf_tensor("bcatT", [ER, D], BF16))
        x1g = E(nc.sbuf_tensor("x1g", [128, CH * X1W], BF16))
        attn = [E(nc.sbuf_tensor(f"attn{c}", [128, T], BF16)) for c in range(4)]
        sums_sb = [E(nc.sbuf_tensor(f"sums_sb{i}", [128, T], F32))
                   for i in range(2)]
        max_sb = E(nc.sbuf_tensor("max_sb", [128, 16], F32))
        PS = E(nc.psum_tensor("PS", [128, 4096], F32))

        block = E(nc.Block())
        ctx = Ctx(nc, base)
        nc.const_aps.register(F32, SQ512, cst[:, 0:1])
        nc.const_aps.register(F32, 2.0, cst[:, 1:2])

        def BK(b, c0=0, c1=512, r0=0, r1=128):
            return PS[r0:r1, 512 * b + c0: 512 * b + c1]

        bank_tok = [None] * 8

        def xb_sl(buf, dc, c0, n, tiles=None):
            return buf[:, dc * XW + c0: dc * XW + c0 + n]

        def strided_tok(buf, dc, chw, j0, ntile, wid=128):
            return buf[:, dc * chw + j0 * 129: dc * chw + (j0 + ntile) * 129] \
                .rearrange("p (a b) -> p a b", b=129)[:, :, 0:wid]

        def setup():
            def v0(v):
                v.memset(eyef[:], 1.0)
                v.memset(onesf_row[:], 1.0)
                v.memset(onesb_col[:], 1.0)
                v.memset(indf[:], 1.0)
                v.memset(eps_col[:], EPS)
                v.memset(cst[:, 0:1], SQ512)
                v.memset(cst[:, 1:2], 2.0)
                v.memset(u0[:, 0:64], 1.0)
                v.memset(u0[:, 64:128], 0.0)
                v.memset(u1[:, 0:64], 0.0)
                v.memset(u1[:, 64:128], 1.0)
                ctx.inc("vS", v.memset(
                    x1g[:, 128: CH * X1W: 129], 1.0))
            block.vector(v0)

            def g0(g):
                ctx.wait(g, "vS")
                ctx.inc("gS", g.affine_select(
                    out=eyef[:], in_=eyef[:], pattern=[[1, 128]],
                    compare_op=ALU.is_equal, fill=0.0,
                    base=0, channel_multiplier=-1))
                ctx.wait(g, "gS")
                ctx.inc("gS", g.affine_select(
                    out=indf[:], in_=indf[:], pattern=[[1, ER]],
                    compare_op=ALU.is_ge, fill=0.0,
                    base=0, channel_multiplier=-RK))
                ctx.wait(g, "gS")
                ctx.inc("gS", g.affine_select(
                    out=indf[:], in_=indf[:], pattern=[[-1, ER]],
                    compare_op=ALU.is_ge, fill=0.0,
                    base=RK - 1, channel_multiplier=RK))
                for (dst, src) in [(qT_im, qtil), (kT_im, ktil),
                                   (vT_im, vtil), (woT_im, wotil)]:
                    ctx.inc("dGw", g.dma_start(out=dst, in_=src), 16)
                ctx.inc("dGw", g.dma_start(out=aextT[:], in_=atil), 16)
                ctx.inc("dGw", g.dma_start(out=bcatT[:], in_=btil), 16)
            block.gpsimd(g0)

            def v1(v):
                ctx.wait(v, "gS")
                ctx.inc("vS", v.tensor_copy(eyeb[:], eyef[:]))
            block.vector(v1)

            def bias_cols(src_row_ap, n, dst_col, dstoff):
                nchunk = n // 128
                def yf(sync):
                    ctx.wait(sync, "vS")
                    ctx.inc("dY", sync.dma_start(
                        out=brow[0:1, 0:min(n, 512)],
                        in_=src_row_ap), 16)
                block.sync(yf)

                def tf(t):
                    ctx.wait(t, "dY")
                    ctx.wait(t, "vS")
                    ctx.wait(t, "gS")
                    for c in range(nchunk):
                        ctx.inc("tS", t.matmul(
                            PS[0:128, 5 * 512 + c: 5 * 512 + c + 1],
                            brow[0:1, c * 128:(c + 1) * 128], eyef[0:1, 0:1],
                            start=(c == 0), stop=(c == nchunk - 1),
                            is_transpose=True))
                block.tensor(tf)

                def vf(v):
                    ctx.wait(v, "tS")
                    ctx.inc("vS", v.tensor_copy(
                        dst_col[:, dstoff:dstoff + nchunk],
                        PS[0:128, 5 * 512: 5 * 512 + nchunk]))
                block.vector(vf)

            bias_cols(qb[None, :], FQ, qb_col, 0)
            bias_cols(kb[None, :], FQ, kb_col, 0)
            bias_cols(vb[None, :], FQ, vb_col, 0)
            for half in range(8):
                bias_cols(wob[None, half * 512:(half + 1) * 512], 512,
                          wob_col, half * 4)

            def t1(t):
                ctx.wait(t, "dGw")
                for dc in range(CH):
                    mm = t.matmul(BK(6, 0, MW, r1=1),
                                  onesb_col[:], aextT[:, dc * MW:(dc + 1) * MW],
                                  start=(dc == 0), stop=(dc == CH - 1))
                ctx.inc("tS", mm)
            block.tensor(t1)

            def v2(v):
                ctx.wait(v, "tS")
                ctx.inc("vS", v.tensor_copy(wsA[:], BK(6, 0, MW, r1=1)))
            block.vector(v2)
        setup()
        snap0 = {n: sv[1] for n, sv in ctx.sems.items()}
        for e in ("tensor", "vector", "scalar", "gpsimd", "sync"):
            def bf0(eng, snap0=snap0):
                for n, v in snap0.items():
                    if v > 0:
                        eng.wait_ge(ctx.sems[n][0], v)
            getattr(block, e)(bf0)

        def ln_gram(src_of, g0, pre_wait=None):
            toks = {}
            def tf(t):
                if pre_wait is not None:
                    pre_wait(t)
                for i in range(2):
                    ctx.wait_tok(t, bank_tok[i])
                    for dc in range(CH):
                        mm = t.matmul(BK(i, 0, 129),
                                      src_of(dc, g0 + i, 128),
                                      src_of(dc, g0 + i, 129),
                                      start=(dc == 0), stop=(dc == CH - 1))
                    ctx.inc("tS", mm)
                    toks[i] = ctx.val("tS")
            block.tensor(tf)
            return toks

        def ln_statsV(g0, gtoks):
            def vf(v):
                for i in range(2):
                    ti = g0 + i
                    ctx.wait(v, "tS", gtoks[i])
                    v.tensor_mul(trash[:], BK(i, 0, 128), eyef[:])
                    v.drain()
                    v.reduce_sum(ss_col[:, ti:ti + 1], trash[:], axis=AX.X)
                    v.drain()
                    ctx.inc("vS", v.tensor_copy(sx_col[:, ti:ti + 1],
                                                BK(i, 128, 129)))
                    bank_tok[i] = ctx.tok("vS")
                sl = slice(g0, g0 + 2)
                v.drain()
                v.tensor_scalar_mul(m_col[:, sl], sx_col[:, sl], -1.0 / D)
                v.tensor_scalar_mul(va_col[:, sl], ss_col[:, sl], 1.0 / D)
                v.drain()
                v.tensor_mul(vb2_col[:, sl], m_col[:, sl], m_col[:, sl])
                v.drain()
                ctx.inc("vS", v.tensor_sub(va_col[:, sl], va_col[:, sl],
                                           vb2_col[:, sl]))
            block.vector(vf)
            return ctx.val("vS")

        def ln_lnexp(g0, vtok):
            def sf(s):
                ctx.wait(s, "vS", vtok)
                s.activation(out=vb2_col[:, g0:g0 + 2], in_=va_col[:, g0:g0 + 2],
                             func=AF.Ln, bias=eps_col[:, 0:1])
                s.drain()
                ctx.inc("sS", s.activation(out=r_col[:, g0:g0 + 2],
                                           in_=vb2_col[:, g0:g0 + 2],
                                           func=AF.Exp, scale=-0.5))
            block.scalar(sf)
            return ctx.val("sS")

        def ln_rowsT(g0, stok):
            b2 = bank_tok[2]
            b3 = bank_tok[3]
            def tf(t):
                ctx.wait(t, "sS", stok)
                ctx.wait_tok(t, b2)
                ctx.wait_tok(t, b3)
                for i in range(2):
                    t.matmul(PS[0:1, 2 * 512 + i * 128: 2 * 512 + (i + 1) * 128],
                             m_col[:, g0 + i:g0 + i + 1], eyef[:],
                             start=(i == 0), stop=(i == 1), is_transpose=True)
                for i in range(2):
                    mm = t.matmul(
                        PS[0:1, 3 * 512 + i * 128: 3 * 512 + (i + 1) * 128],
                        r_col[:, g0 + i:g0 + i + 1], eyef[:],
                        start=(i == 0), stop=(i == 1), is_transpose=True)
                ctx.inc("tS", mm)
            block.tensor(tf)
            return ctx.val("tS")

        def ln_rowsV(g0, rtok):
            def vf(v):
                ctx.wait(v, "tS", rtok)
                t0 = g0 * 128
                v.tensor_copy(mrow[0:1, t0:t0 + 256], PS[0:1, 2 * 512:2 * 512 + 256])
                ctx.inc("vS", v.tensor_copy(rrow[0:1, t0:t0 + 256],
                                            PS[0:1, 3 * 512:3 * 512 + 256]))
            block.vector(vf)
            bank_tok[2] = ctx.tok("vS")
            bank_tok[3] = ctx.tok("vS")
            return ctx.val("vS")

        def mid_main(tb, src_mm):
            def tf(t):
                ctx.wait_tok(t, bank_tok[5])
                for dc in range(CH):
                    t.matmul(BK(5, 0, TB, r1=MW),
                             aextT[:, dc * MW:(dc + 1) * MW],
                             src_mm(dc), start=(dc == 0), stop=False)
            block.tensor(tf)

        def ffn_gate_delta(tb, src_res, evac_s, evac_v, v_mrow):
            t0 = tb * TB
            g0 = tb * 2
            sv = ctx.val("sS")
            vv = ctx.val("vS")
            b2 = bank_tok[2]
            b4 = [bank_tok[4 + k] for k in range(4)]
            b5 = bank_tok[5]
            ttoks = {}

            def tf(t):
                ctx.wait(t, "vS", v_mrow)
                mm = t.matmul(BK(5, 0, TB, r1=MW), wsA[:],
                              mrow[0:1, t0:t0 + TB], start=False, stop=True)
                ctx.inc("tS", mm)
                ttoks["aug"] = ctx.val("tS")
                ctx.wait_tok(t, b2)
                ctx.wait(t, "vS", vv + 1)
                for i in range(2):
                    mm = t.matmul(
                        PS[0:128, 2 * 512 + i * 4: 2 * 512 + (i + 1) * 4],
                        lrow[0:NE, i * 128:(i + 1) * 128], eyef[0:NE, 0:NE],
                        start=(i == 0), stop=(i == 1), is_transpose=True)
                ctx.inc("tS", mm)
                ttoks["lT"] = ctx.val("tS")
                ctx.wait(t, "vS", vv + 3)
                for i in range(2):
                    mm = t.matmul(
                        PS[0:NE, 2 * 512 + 128 + i * 128:
                           2 * 512 + 128 + (i + 1) * 128],
                        gsc[:, 52 + i * 4:52 + (i + 1) * 4], eyef[:],
                        start=(i == 0), stop=(i == 1), is_transpose=True)
                ctx.inc("tS", mm)
                ttoks["wT"] = ctx.val("tS")
                ctx.wait(t, "vS", vv + 4)
                ctx.wait(t, "sS", sv + 1)
                mm = t.matmul(BK(5, 0, TB, r1=ER), indf[:], wrow[:],
                              start=True, stop=True)
                ctx.inc("tS", mm)
                ttoks["ind"] = ctx.val("tS")
                ctx.wait(t, "vS", vv + 5)
                for oc in range(CH):
                    bkk = 4 + oc % 4
                    if oc >= 4:
                        prev = oc - 4
                        if prev % 2 == 0:
                            ctx.wait(t, "sS", sv + 2 + prev // 2 + 1)
                        else:
                            ctx.wait(t, "vS", vv + 5 + prev // 2 + 1)
                    else:
                        ctx.wait_tok(t, b4[oc])
                    t.matmul(BK(bkk, 0, TB), bcatT[:, oc * 128:(oc + 1) * 128],
                             gm_sb[:], start=True, stop=False)
                    mm = t.matmul(BK(bkk, 0, TB), eyeb[:], src_res(oc),
                                  start=False, stop=True)
                    ctx.inc("tS", mm)
                    ttoks[("d", oc)] = ctx.val("tS")
            block.tensor(tf)

            def sf(s):
                ctx.wait(s, "tS", ttoks["aug"])
                ctx.inc("sS", s.copy(mid_sb[:], BK(5, 0, TB, r1=ER)))
                ctx.wait(s, "vS", vv + 2)
                ctx.inc("sS", s.activation(out=gsc[:, 24:32], in_=ltok[:],
                                           func=AF.Exp))
                for oc in range(0, CH, 2):
                    ctx.wait(s, "tS", ttoks[("d", oc)])
                    evac_s(s, oc, 4 + oc % 4)
            block.scalar(sf)
            assert ctx.val("sS") == sv + 2 + 16, (ctx.val("sS"), sv + 18)

            def vf(v):
                ctx.wait(v, "tS", ttoks["aug"])
                ctx.inc("vS", v.tensor_copy(lrow[:], PS[ER:MW, 5 * 512:5 * 512 + TB]))
                ctx.wait(v, "tS", ttoks["lT"])
                v.tensor_copy(ltok[:], PS[0:128, 2 * 512:2 * 512 + 8])
                v.drain()
                for i in range(2):
                    v.tensor_single_scalar(ltok[:, i * 4:(i + 1) * 4],
                                           ltok[:, i * 4:(i + 1) * 4],
                                           r_col[:, g0 + i:g0 + i + 1],
                                           ALU.mult)
                v.drain()
                l_ = [ltok[:, e: 2 * NE: NE] for e in range(NE)]
                v.tensor_max(gsc[:, 0:2], l_[0], l_[1])
                v.tensor_tensor(gsc[:, 2:4], l_[0], l_[1], ALU.min)
                v.tensor_max(gsc[:, 4:6], l_[2], l_[3])
                v.tensor_tensor(gsc[:, 6:8], l_[2], l_[3], ALU.min)
                v.drain()
                v.tensor_tensor(gsc[:, 8:10], gsc[:, 0:2], gsc[:, 4:6], ALU.min)
                v.tensor_max(gsc[:, 10:12], gsc[:, 2:4], gsc[:, 6:8])
                v.drain()
                v.tensor_max(gsc[:, 12:14], gsc[:, 8:10], gsc[:, 10:12])
                v.drain()
                for e in range(NE):
                    v.tensor_tensor(gsc[:, 16 + e: 24: NE], l_[e],
                                    gsc[:, 12:14], ALU.is_ge)
                ctx.inc("vS", v.drain())
                ctx.wait(v, "sS", sv + 2)
                v.tensor_mul(gsc[:, 32:40], gsc[:, 24:32], gsc[:, 16:24])
                v.drain()
                v.tensor_add(gsc[:, 40:42], gsc[:, 32: 40: NE], gsc[:, 33: 40: NE])
                v.drain()
                v.tensor_add(gsc[:, 42:44], gsc[:, 34: 40: NE], gsc[:, 35: 40: NE])
                v.drain()
                v.tensor_add(gsc[:, 40:42], gsc[:, 40:42], gsc[:, 42:44])
                v.drain()
                v.tensor_scalar_mul(gsc[:, 44:46], gsc[:, 40:42], 0.5)
                v.drain()
                v.reciprocal(gsc[:, 46:48], gsc[:, 44:46])
                v.drain()
                for i in range(2):
                    v.tensor_mul(gsc[:, 48 + i:49 + i], gsc[:, 46 + i:47 + i],
                                 r_col[:, g0 + i:g0 + i + 1])
                v.drain()
                for e in range(NE):
                    v.tensor_mul(gsc[:, 52 + e: 60: NE], gsc[:, 32 + e: 40: NE],
                                 gsc[:, 48:50])
                ctx.inc("vS", v.drain())
                ctx.wait(v, "tS", ttoks["wT"])
                ctx.inc("vS", v.tensor_copy(
                    wrow[:], PS[0:NE, 2 * 512 + 128: 2 * 512 + 128 + TB]))
                ctx.wait(v, "tS", ttoks["ind"])
                ctx.inc("vS", v.tensor_mul(gm_sb[:], mid_sb[:],
                                           BK(5, 0, TB, r1=ER)))
                for oc in range(1, CH, 2):
                    ctx.wait(v, "tS", ttoks[("d", oc)])
                    evac_v(v, oc, 4 + oc % 4)
            block.vector(vf)
            assert ctx.val("vS") == vv + 5 + 16, (ctx.val("vS"), vv + 21)

            bank_tok[2] = ("vS", vv + 4)
            bank_tok[5] = ("vS", vv + 5)
            for oc in range(CH - 4, CH):
                bkk = 4 + oc % 4
                if oc % 2 == 0:
                    bank_tok[bkk] = ("sS", sv + 2 + oc // 2 + 1)
                else:
                    bank_tok[bkk] = ("vS", vv + 5 + oc // 2 + 1)
            return ttoks

        p1 = ExitStack()
        xbuf = [p1.enter_context(nc.sbuf_tensor(f"xb{i}", [128, CH * XW], BF16))
                for i in range(2)]
        wt = [p1.enter_context(nc.sbuf_tensor(f"wt{i}", [128, D], BF16))
              for i in range(2)]
        vt_g = p1.enter_context(nc.sbuf_tensor("vt_g", [128, 4 * PG], BF16))
        qkst = [p1.enter_context(nc.sbuf_tensor(f"qkst{i}", [128, PG], BF16))
                for i in range(2)]

        xb_cons = [None, None]
        wt_cons = [None, None]
        qkst_cons = [None, None]

        def v_ones_init(v):
            for b in range(2):
                v.memset(xbuf[b][:, 128: CH * XW: 129], 1.0)
            ctx.inc("vS", v.drain())
        block.vector(v_ones_init)

        def emit_xload(tb):
            if tb >= 8:
                return
            b = tb % 2
            def gf(g, tb=tb, b=b):
                ctx.wait_tok(g, xb_cons[b])
                for a in range(2):
                    dst = xbuf[b].rearrange("p (dc w) -> p dc w", w=XW)[
                        :, :, a * 129: a * 129 + 128]
                    src = xT[:, tb * TB + a * 128: tb * TB + (a + 1) * 128] \
                        .rearrange("(dc p) t -> p dc t", p=128)
                    ctx.inc("dGx%d" % b, g.dma_start(out=dst, in_=src), 16)
            block.gpsimd(gf)

        x_ld = {}
        emit_xload(0)
        x_ld[0] = ctx.tok("dGx0")
        emit_xload(1)
        x_ld[1] = ctx.tok("dGx1")

        wim = [qT_im] * 4 + [kT_im] * 4 + [vT_im] * 4
        wt_ld = {}

        def emit_wtload(pg, fi):
            if fi >= 12:
                return
            b = fi % 2
            c = fi % 4
            def sf(s, pg=pg, fi=fi, b=b, c=c):
                ctx.wait_tok(s, wt_cons[b])
                ctx.inc("dT%d" % b, s.dma_start(out=wt[b][:], in_=wim[fi][c]), 16)
            block.scalar(sf)
            wt_ld[(pg, fi)] = ctx.tok("dT%d" % b)

        def p1_evacs(tb, j):
            def evac_s(s, oc, bkk, j=j):
                dst = x1g[:, oc * X1W + j * 258: oc * X1W + (j + 1) * 258] \
                    .rearrange("p (a b) -> p a b", b=129)[:, :, 0:128]
                ctx.wait(s, "dY")
                ctx.inc("sS", s.copy(dst, BK(bkk, 0, TB)
                                     .rearrange("p (a b) -> p a b", b=128)))
            def evac_v(v, oc, bkk, j=j):
                dst = x1g[:, oc * X1W + j * 258: oc * X1W + (j + 1) * 258] \
                    .rearrange("p (a b) -> p a b", b=129)[:, :, 0:128]
                ctx.wait(v, "dY")
                ctx.inc("vS", v.tensor_copy(dst, BK(bkk, 0, TB)
                                            .rearrange("p (a b) -> p a b", b=128)))
            return evac_s, evac_v

        def emit_gram2_finish(prev):
            vtok2 = ln_statsV(prev["g0"], prev["g2toks"])
            stok2 = ln_lnexp(prev["g0"], vtok2)
            rtok2 = ln_rowsT(prev["g0"], stok2)
            ln_rowsV(prev["g0"], rtok2)
            def yf(sync, tb=prev["tb"], j=prev["j"]):
                ctx.wait(sync, "sS")
                ctx.wait(sync, "vS")
                for a2 in range(2):
                    a = j * 2 + a2
                    src = x1g.rearrange("p (dc w) -> p dc w", w=X1W)[
                        :, :, a * 129: a * 129 + 128]
                    dst = x1_dram[:, tb * TB + a2 * 128:
                                  tb * TB + (a2 + 1) * 128].rearrange(
                        "(dc p) t -> p dc t", p=128)
                    ctx.inc("dY", sync.dma_start(out=dst, in_=src), 16)
            block.sync(yf)

        def mk_srcs(tb):
            xb = xbuf[tb % 2]
            def src_x(dc, ti, n, xb=xb, tb=tb):
                lo = dc * XW + (ti - tb * 2) * 129
                return xb[:, lo: lo + n]
            def src_mm(dc, xb=xb):
                return strided_tok(xb, dc, XW, 0, 2, 128)
            def src_res(oc, xb=xb):
                return strided_tok(xb, oc, XW, 0, 2, 128)
            def pre(t, tb=tb):
                ctx.wait_tok(t, x_ld[tb])
                if tb == 0:
                    ctx.wait(t, "vS")
            return src_x, src_mm, src_res, pre

        prev = None
        nxt = None
        for pg in range(2):
            for j in range(4):
                tb = pg * 4 + j
                src_x, src_mm, src_res, pre = mk_srcs(tb)
                if nxt is None:
                    g1 = ln_gram(src_x, tb * 2, pre_wait=pre)
                    mid_main(tb, src_mm)
                else:
                    g1 = nxt["g1"]
                    if not nxt["mid"]:
                        mid_main(tb, src_mm)
                vtok1 = ln_statsV(tb * 2, g1)
                stok1 = ln_lnexp(tb * 2, vtok1)
                if prev is not None:
                    def pre2(t, prev=prev):
                        ctx.wait_tok(t, prev["etok_s"])
                        ctx.wait_tok(t, prev["etok_v"])
                    g2 = ln_gram(prev["srcg2"], prev["g0"], pre_wait=pre2)
                    vtok2 = ln_statsV(prev["g0"], g2)
                    stok2 = ln_lnexp(prev["g0"], vtok2)
                rtok1 = ln_rowsT(tb * 2, stok1)
                v_mrow = ln_rowsV(tb * 2, rtok1)
                g1n = None
                if tb + 1 < 8:
                    src_xn, src_mmn, _, pren = mk_srcs(tb + 1)
                    g1n = ln_gram(src_xn, (tb + 1) * 2, pre_wait=pren)
                if prev is not None:
                    rtok2 = ln_rowsT(prev["g0"], stok2)
                    ln_rowsV(prev["g0"], rtok2)
                    def yf(sync, tbp=prev["tb"], jp=prev["j"]):
                        ctx.wait(sync, "sS")
                        ctx.wait(sync, "vS")
                        for a2 in range(2):
                            aa = jp * 2 + a2
                            src = x1g.rearrange("p (dc w) -> p dc w", w=X1W)[
                                :, :, aa * 129: aa * 129 + 128]
                            dst = x1_dram[:, tbp * TB + a2 * 128:
                                          tbp * TB + (a2 + 1) * 128].rearrange(
                                "(dc p) t -> p dc t", p=128)
                            ctx.inc("dY", sync.dma_start(out=dst, in_=src), 16)
                    block.sync(yf)
                ev_s, ev_v = p1_evacs(tb, j)
                ffn_gate_delta(tb, src_res, ev_s, ev_v, v_mrow)
                xb_cons[tb % 2] = ctx.tok("tS")
                mid_ahead = (tb + 1 < 8) and (j != 3)
                if mid_ahead:
                    mid_main(tb + 1, src_mmn)
                emit_xload(tb + 2)
                if tb + 2 < 8:
                    x_ld[tb + 2] = ctx.tok("dGx%d" % (tb % 2))

                def srcg2(dc, ti, n, pg=pg):
                    lo = dc * X1W + (ti - pg * 8) * 129
                    return x1g[:, lo: lo + n]
                prev = {"tb": tb, "j": j, "g0": tb * 2, "srcg2": srcg2,
                        "etok_s": ctx.tok("sS"), "etok_v": ctx.tok("vS")}
                nxt = ({"g1": g1n, "mid": mid_ahead}
                       if g1n is not None else None)
            def pre3(t, prev=prev):
                ctx.wait_tok(t, prev["etok_s"])
                ctx.wait_tok(t, prev["etok_v"])
            g2 = ln_gram(prev["srcg2"], prev["g0"], pre_wait=pre3)
            vtok2 = ln_statsV(prev["g0"], g2)
            stok2 = ln_lnexp(prev["g0"], vtok2)
            rtok2 = ln_rowsT(prev["g0"], stok2)
            ln_rowsV(prev["g0"], rtok2)
            def yft(sync, tbp=prev["tb"], jp=prev["j"]):
                ctx.wait(sync, "sS")
                ctx.wait(sync, "vS")
                for a2 in range(2):
                    aa = jp * 2 + a2
                    src = x1g.rearrange("p (dc w) -> p dc w", w=X1W)[
                        :, :, aa * 129: aa * 129 + 128]
                    dst = x1_dram[:, tbp * TB + a2 * 128:
                                  tbp * TB + (a2 + 1) * 128].rearrange(
                        "(dc p) t -> p dc t", p=128)
                    ctx.inc("dY", sync.dma_start(out=dst, in_=src), 16)
            block.sync(yft)
            prev = None

            def trx(t, pg=pg):
                ctx.wait(t, "vS")
                ctx.wait_tok(t, bank_tok[6])
                ctx.wait_tok(t, bank_tok[7])
                for jb in range(2):
                    ctx.inc("tS", t.matmul(
                        BK(6 + jb), onesf_row[:],
                        rrow[0:1, pg * PG + jb * 512: pg * PG + (jb + 1) * 512],
                        start=True, stop=True))
            block.tensor(trx)

            def vrx(v):
                ctx.wait(v, "tS")
                for jb in range(2):
                    last = v.tensor_copy(rexp[:, jb * 512:(jb + 1) * 512],
                                         BK(6 + jb))
                ctx.inc("vS", last)
            block.vector(vrx)
            bank_tok[6] = ctx.tok("vS")
            bank_tok[7] = ctx.tok("vS")

            emit_wtload(pg, 0)
            emit_wtload(pg, 1)
            for fi in range(12):
                kind = fi // 4
                c = fi % 4
                wtb = wt[fi % 2]
                pair = (4, 5) if fi % 2 == 0 else (6, 7)

                if pg == 0:
                    def tws(t, fi=fi, wtb=wtb):
                        ctx.wait_tok(t, wt_ld[(0, fi)])
                        ctx.wait_tok(t, bank_tok[2])
                        for dc in range(CH):
                            mm = t.matmul(BK(2, 0, 128, r1=1), onesb_col[:],
                                          wtb[:, dc * 128:(dc + 1) * 128],
                                          start=(dc == 0), stop=(dc == CH - 1))
                        ctx.inc("tS", mm)
                    block.tensor(tws)

                    def vws(v, fi=fi):
                        ctx.wait(v, "tS")
                        ctx.inc("vS", v.tensor_copy(
                            wsQKV[0:1, fi * 128:(fi + 1) * 128],
                            BK(2, 0, 128, r1=1)))
                    block.vector(vws)
                    bank_tok[2] = ctx.tok("vS")

                def tp(t, fi=fi, pg=pg, wtb=wtb, pair=pair):
                    ctx.wait_tok(t, wt_ld[(pg, fi)])
                    ctx.wait(t, "vS")
                    ctx.wait_tok(t, bank_tok[pair[0]])
                    ctx.wait_tok(t, bank_tok[pair[1]])
                    for dc in range(CH):
                        for jb in range(2):
                            t.matmul(BK(pair[jb]),
                                     wtb[:, dc * 128:(dc + 1) * 128],
                                     strided_tok(x1g, dc, X1W, jb * 4, 4, 128),
                                     start=(dc == 0), stop=False)
                    for jb in range(2):
                        ctx.inc("tS", t.matmul(
                            BK(pair[jb]), wsQKV[0:1, fi * 128:(fi + 1) * 128],
                            mrow[0:1, pg * PG + jb * 512: pg * PG + (jb + 1) * 512],
                            start=False, stop=True))
                block.tensor(tp)
                wt_cons[fi % 2] = ctx.tok("tS")

                emit_wtload(pg, fi + 2)

                def vp(v, fi=fi, pair=pair):
                    ctx.wait(v, "tS")
                    ctx.wait(v, "sS")
                    ctx.wait_tok(v, qkst_cons[fi % 2])
                    v.tensor_mul(tmp_f[:, 0:512], BK(pair[0]), rexp[:, 0:512])
                    v.drain()
                    ctx.inc("vS", v.tensor_mul(tmp_f[:, 512:1024], BK(pair[1]),
                                               rexp[:, 512:1024]))
                block.vector(vp)
                bank_tok[pair[0]] = ctx.tok("vS")
                bank_tok[pair[1]] = ctx.tok("vS")

                bcol = [qb_col, kb_col, vb_col][kind]
                def sp(s, fi=fi, c=c, kind=kind, bcol=bcol):
                    ctx.wait(s, "vS")
                    if kind == 2:
                        dstap = vt_g[:, c * PG:(c + 1) * PG]
                    else:
                        dstap = qkst[fi % 2][:]
                    ctx.inc("sS", s.activation(out=dstap, in_=tmp_f[:, 0:1024],
                                               func=AF.Identity,
                                               bias=bcol[:, c:c + 1]))
                block.scalar(sp)

                if kind != 2:
                    def yq(sync, fi=fi, c=c, kind=kind, pg=pg):
                        ctx.wait(sync, "sS")
                        dstd = (q_dram if kind == 0 else k_dram)
                        ctx.inc("dQ%d" % (fi % 2), sync.dma_start(
                            out=dstd[c * 128:(c + 1) * 128,
                                     pg * PG:(pg + 1) * PG],
                            in_=qkst[fi % 2][:]), 16)
                    block.sync(yq)
                    qkst_cons[fi % 2] = ctx.tok("dQ%d" % (fi % 2))

            def yv(sync, pg=pg):
                ctx.wait(sync, "sS")
                for fc in range(4):
                    ctx.inc("dY", sync.dma_start(
                        out=v_dram[fc * 128:(fc + 1) * 128,
                                   pg * PG:(pg + 1) * PG],
                        in_=vt_g[:, fc * PG:(fc + 1) * PG]), 16)
            block.sync(yv)

        if "x1" in dbg_t:
            def ydbg(sync):
                ctx.wait(sync, "sS")
                ctx.wait(sync, "vS")
                ctx.wait(sync, "dY")
                src = x1_dram
                ctx.inc("dY", sync.dma_start(out=dbg_t["x1"], in_=src), 16)
            block.sync(ydbg)
        for nm, dr in (("q", q_dram), ("k", k_dram), ("v", v_dram)):
            if nm in dbg_t:
                def ydbg2(sync, nm=nm, dr=dr):
                    ctx.wait(sync, "dY")
                    ctx.wait(sync, "dQ0")
                    ctx.wait(sync, "dQ1")
                    ctx.inc("dY", sync.dma_start(out=dbg_t[nm], in_=dr), 16)
                block.sync(ydbg2)
        if "mrow" in dbg_t:
            def ydbg3(sync):
                ctx.wait(sync, "vS")
                ctx.inc("dY", sync.dma_start(out=dbg_t["mrow"], in_=mrow[:]), 16)
            block.sync(ydbg3)
        if "rrow" in dbg_t:
            def ydbg4(sync):
                ctx.wait(sync, "vS")
                ctx.inc("dY", sync.dma_start(out=dbg_t["rrow"], in_=rrow[:]), 16)
            block.sync(ydbg4)

        snap = {n: s[1] for n, s in ctx.sems.items()}
        for e in ("tensor", "vector", "scalar", "gpsimd", "sync"):
            def bf(eng, snap=snap):
                for n, v in snap.items():
                    if v > 0:
                        eng.wait_ge(ctx.sems[n][0], v)
            getattr(block, e)(bf)
        p1.close()

        p2 = ExitStack()
        v_sb = p2.enter_context(nc.sbuf_tensor("v_sb", [128, 16 * 640], BF16))
        Qaug = p2.enter_context(nc.sbuf_tensor("Qaug", [65, T], BF16))
        Kaug = p2.enter_context(nc.sbuf_tensor("Kaug", [65, T], BF16))
        probs = [p2.enter_context(nc.sbuf_tensor(f"probs{i}", [128, PG], BF16))
                 for i in range(2)]
        vstage = [p2.enter_context(nc.sbuf_tensor(f"vstage{i}", [128, FQ], BF16))
                  for i in range(2)]
        rstg = [p2.enter_context(nc.sbuf_tensor(f"rstg{i}", [1, T], BF16))
                for i in range(2)]
        mx2 = p2.enter_context(nc.sbuf_tensor("mx2", [128, 2], F32))

        def gx1(g):
            for a in range(8):
                dst = x1g.rearrange("p (dc w) -> p dc w", w=X1W)[
                    :, :, a * 129: a * 129 + 128]
                src = x1_dram[:, a * 128:(a + 1) * 128].rearrange(
                    "(dc p) t -> p dc t", p=128)
                ctx.inc("dGx0", g.dma_start(out=dst, in_=src), 16)
        block.gpsimd(gx1)
        x1_ld_pg0 = ctx.tok("dGx0")

        def p2v0(v):
            for kt in range(16):
                for h in range(NH):
                    v.memset(v_sb[:, kt * 640 + h * 80 + 64:
                                  kt * 640 + h * 80 + 65], 1.0)
            v.memset(Kaug[64:65, :], -1.0)
            ctx.inc("vS", v.drain())
        block.vector(p2v0)


        QA = [Qaug, p2.enter_context(nc.sbuf_tensor("Qaug1", [65, T], BF16))]
        KA = [Kaug, p2.enter_context(nc.sbuf_tensor("Kaug1", [65, T], BF16))]

        def p2v0b(v):
            ctx.inc("vS", v.memset(KA[1][64:65, :], -1.0))
        block.vector(p2v0b)

        ttok_prev = {}
        qk_ld = [None, None]
        for i in range(NH + 1):
            hA = i if i < NH else None
            hB = i - 1 if i >= 1 else None
            bi = i % 2
            bj = (i - 1) % 2
            dy_pre = ctx.tok("dY")

            if hA is not None:
                def ya(sync, hA=hA, bi=bi, tprev=dict(ttok_prev)):
                    if ("B", 31) in tprev:
                        ctx.wait(sync, "tS", tprev[("B", 31)])
                    else:
                        ctx.wait(sync, "tS")
                    ctx.inc("dY", sync.dma_start(
                        out=QA[bi][0:64, :],
                        in_=q_dram[hA * 64:(hA + 1) * 64, :]), 16)
                    ctx.inc("dY", sync.dma_start(
                        out=KA[bi][0:64, :],
                        in_=k_dram[hA * 64:(hA + 1) * 64, :]), 16)
                block.sync(ya)
                qk_ld[bi] = ctx.tok("dY")

            vbase = ctx.val("vS")
            vtok = {}
            c = vbase
            if hA is not None:
                for q in range(8):
                    c += 1; vtok[("rm1", q)] = c
                    c += 1; vtok[("rm2", q)] = c
                    c += 1; vtok[("mx", q)] = c
                    if i == 0:
                        c += 1; vtok[("cp", q)] = c
            if hB is not None:
                c += 1; vtok[("vo", 0)] = c
                c += 1; vtok[("so", 0)] = c
            if hA is not None:
                for q in range(8, 16):
                    c += 1; vtok[("rm1", q)] = c
                    c += 1; vtok[("rm2", q)] = c
                    c += 1; vtok[("mx", q)] = c
                    if i == 0:
                        c += 1; vtok[("cp", q)] = c
                c += 1; vtok[("vc",)] = c
            if hB is not None:
                c += 1; vtok[("vo", 1)] = c
                c += 1; vtok[("so", 1)] = c
            vend = c

            dtr = {}
            if i == 0:
                def p2s0b(s, vtok=vtok, dtr=dtr):
                    for kt in range(16):
                        if kt < 2:
                            ctx.wait_tok(s, dy_pre)
                        else:
                            ctx.wait(s, "vS", vtok[("cp", kt - 2)])
                        ctx.inc("dT%d" % (kt % 2), s.dma_start(
                            out=vstage[kt % 2][:],
                            in_=v_dram[:, kt * 128:(kt + 1) * 128],
                            transpose=True), 16)
                        dtr[kt] = ctx.tok("dT%d" % (kt % 2))
                block.scalar(p2s0b)
            sbase = ctx.val("sS")
            stok = {j: sbase + j + 1 for j in range(32)} if hB is not None else {}
            send = sbase + (32 if hB is not None else 0)

            ttok = {}

            def tbody(t, hA=hA, hB=hB, bi=bi, bj=bj, vtok=vtok, stok=stok,
                      vbase=vbase, ttok=ttok, qk=dict(q0=qk_ld[bi])):
                for step in range(16):
                    if step == 0:
                        ctx.wait(t, "vS", vbase)
                    if hA is not None:
                        if step == 0:
                            ctx.wait_tok(t, qk["q0"])
                        else:
                            ctx.wait(t, "vS", vtok[("rm2", step - 1)])
                        for kb_ in range(2):
                            mm = t.matmul(BK(kb_),
                                          QA[bi][0:64, step * 128:(step + 1) * 128],
                                          KA[bi][0:64, kb_ * 512:(kb_ + 1) * 512],
                                          start=True, stop=True)
                        ctx.inc("tS", mm)
                        ttok[("A1", step)] = ctx.val("tS")
                    for half in range(2):
                        if hB is not None:
                            jj = 2 * step + half
                            qh, kt = jj // 16, jj % 16
                            pr = 4 + (jj % 2) * 2
                            if jj >= 2:
                                ctx.wait(t, "sS", stok[jj - 2])
                            if qh == 1 and kt == 0:
                                ctx.wait(t, "vS", vtok[("so", 0)])
                            for js in range(2):
                                mm = t.matmul(
                                    BK(pr + js),
                                    KA[bj][0:65, kt * 128:(kt + 1) * 128],
                                    QA[bj][0:65, qh * PG + js * 512:
                                           qh * PG + (js + 1) * 512],
                                    start=True, stop=True)
                            ctx.inc("tS", mm)
                            ttok[("B", jj)] = ctx.val("tS")
                            ctx.wait(t, "sS", stok[jj])
                            for js in range(2):
                                mm = t.matmul(
                                    PS[0:65, (2 + js) * 512:(2 + js) * 512 + 512],
                                    v_sb[:, kt * 640 + hB * 80:
                                         kt * 640 + hB * 80 + 65],
                                    probs[jj % 2][:, js * 512:(js + 1) * 512],
                                    start=(kt == 0), stop=(kt == 15))
                            ctx.inc("tS", mm)
                            ttok[("AV", jj)] = ctx.val("tS")
                        if hA is not None and half == 0:
                            ctx.wait(t, "vS", vtok[("rm1", step)])
                            for kb_ in range(2):
                                mm = t.matmul(
                                    BK(kb_),
                                    QA[bi][0:64, step * 128:(step + 1) * 128],
                                    KA[bi][0:64, 1024 + kb_ * 512:
                                           1024 + (kb_ + 1) * 512],
                                    start=True, stop=True)
                            ctx.inc("tS", mm)
                            ttok[("A2", step)] = ctx.val("tS")
                if hA is not None:
                    ctx.wait(t, "vS", vtok[("mx", 15)])
                    if hB is not None:
                        ctx.wait(t, "sS", stok[31])
                    for qt in range(16):
                        bkk = 4 + qt // 4
                        mm = t.matmul(
                            PS[0:1, bkk * 512 + (qt % 4) * 128:
                               bkk * 512 + (qt % 4 + 1) * 128],
                            max_sb[:, qt:qt + 1], eyef[:],
                            start=(qt % 4 == 0), stop=(qt % 4 == 3),
                            is_transpose=True)
                        if qt % 4 == 3:
                            ctx.inc("tS", mm)
                    ttok[("cT",)] = ctx.val("tS")
            block.tensor(tbody)

            if hB is not None:
                def sbody(s, hB=hB, ttok=ttok, stok=stok):
                    for jj in range(32):
                        ctx.wait(s, "tS", ttok[("B", jj)])
                        if jj >= 2:
                            ctx.wait(s, "tS", ttok[("AV", jj - 2)])
                        pr = 4 + (jj % 2) * 2
                        ctx.inc("sS", s.activation(
                            out=probs[jj % 2][:],
                            in_=PS[0:128, pr * 512:pr * 512 + 1024],
                            func=AF.Exp))
                block.scalar(sbody)
                assert ctx.val("sS") == send

            def vbody(v, hA=hA, hB=hB, bi=bi, ttok=ttok, i=i, dtr=dtr):
                def rmq(q):
                    ctx.wait(v, "tS", ttok[("A1", q)])
                    v.reduce_max(mx2[:, 0:1], PS[0:128, 0:1024], axis=AX.X)
                    ctx.inc("vS", v.drain())
                    ctx.wait(v, "tS", ttok[("A2", q)])
                    v.reduce_max(mx2[:, 1:2], PS[0:128, 0:1024], axis=AX.X)
                    ctx.inc("vS", v.drain())
                    ctx.inc("vS", v.tensor_max(max_sb[:, q:q + 1],
                                               mx2[:, 0:1], mx2[:, 1:2]))
                    if i == 0:
                        ctx.wait_tok(v, dtr[q])
                        dst = v_sb[:, q * 640:(q + 1) * 640].rearrange(
                            "p (h e) -> p h e", h=NH)[:, :, 0:64]
                        src = vstage[q % 2].rearrange("p (h e) -> p h e", h=NH)
                        ctx.inc("vS", v.tensor_copy(dst, src))
                if hA is not None:
                    for q in range(8):
                        rmq(q)
                if hB is not None:
                    ctx.wait(v, "tS", ttok[("AV", 15)])
                    r0 = (hB % 2) * 64
                    ctx.inc("vS", v.tensor_copy(
                        attn[hB // 2][r0:r0 + 64, 0:PG],
                        PS[0:64, 2 * 512: 2 * 512 + 1024]))
                    r1 = (hB % 4) * 32
                    ctx.inc("vS", v.tensor_scalar_mul(
                        sums_sb[hB // 4][r1:r1 + 1, 0:PG],
                        PS[64:65, 2 * 512: 2 * 512 + 1024], SQ512))
                if hA is not None:
                    for q in range(8, 16):
                        rmq(q)
                    ctx.wait(v, "tS", ttok[("cT",)])
                    for g4 in range(4):
                        last = v.tensor_copy(
                            QA[bi][64:65, g4 * 512:(g4 + 1) * 512],
                            PS[0:1, (4 + g4) * 512:(4 + g4) * 512 + 512])
                    ctx.inc("vS", last)
                if hB is not None:
                    ctx.wait(v, "tS", ttok[("AV", 31)])
                    r0 = (hB % 2) * 64
                    ctx.inc("vS", v.tensor_copy(
                        attn[hB // 2][r0:r0 + 64, PG:2 * PG],
                        PS[0:64, 2 * 512: 2 * 512 + 1024]))
                    r1 = (hB % 4) * 32
                    ctx.inc("vS", v.tensor_scalar_mul(
                        sums_sb[hB // 4][r1:r1 + 1, PG:2 * PG],
                        PS[64:65, 2 * 512: 2 * 512 + 1024], SQ512))
            block.vector(vbody)
            assert ctx.val("vS") == vend, (ctx.val("vS"), vend)
            ttok_prev = ttok
            ttok_prev = ttok

        def vr(v):
            ctx.wait(v, "vS")
            ctx.wait(v, "sS")
            ctx.wait(v, "dY")
            ctx.inc("vS", v.drain())
        block.vector(vr)

        for c in range(4):
            def vstg(v, c=c):
                ctx.wait(v, "vS")
                h0, h1 = 2 * c, 2 * c + 1
                v.tensor_copy(
                    rstg[0][:],
                    sums_sb[h0 // 4][(h0 % 4) * 32:(h0 % 4) * 32 + 1, :])
                ctx.inc("vS", v.tensor_copy(
                    rstg[1][:],
                    sums_sb[h1 // 4][(h1 % 4) * 32:(h1 % 4) * 32 + 1, :]))
            block.vector(vstg)
            for jb in range(4):
                def te(t, c=c, jb=jb):
                    ctx.wait(t, "vS")
                    bkk = jb % 2
                    t.matmul(BK(bkk), u0[:],
                             rstg[0][0:1, jb * 512:(jb + 1) * 512],
                             start=True, stop=False)
                    ctx.inc("tS", t.matmul(
                        BK(bkk), u1[:],
                        rstg[1][0:1, jb * 512:(jb + 1) * 512],
                        start=False, stop=True))
                block.tensor(te)

                def ve(v, c=c, jb=jb):
                    ctx.wait(v, "tS")
                    bkk = jb % 2
                    v.reciprocal(tmp_f[:, 0:512], BK(bkk))
                    v.drain()
                    ctx.inc("vS", v.tensor_mul(
                        attn[c][:, jb * 512:(jb + 1) * 512],
                        attn[c][:, jb * 512:(jb + 1) * 512],
                        tmp_f[:, 0:512]))
                block.vector(ve)

        if "probs" in dbg_t:
            def ypr(sync):
                ctx.wait(sync, "sS")
                ctx.wait(sync, "tS")
                ctx.inc("dY", sync.dma_start(out=dbg_t["probs"],
                                             in_=probs[1][:]), 16)
            block.sync(ypr)
        if "qaug" in dbg_t:
            def yqa(sync):
                ctx.wait(sync, "vS")
                ctx.wait(sync, "tS")
                ctx.inc("dY", sync.dma_start(out=dbg_t["qaug"], in_=Qaug[:]), 16)
            block.sync(yqa)
        if "maxs" in dbg_t:
            def yma(sync):
                ctx.wait(sync, "vS")
                ctx.inc("dY", sync.dma_start(out=dbg_t["maxs"], in_=max_sb[:]), 16)
            block.sync(yma)
        if "vstg" in dbg_t:
            def yvg(sync):
                ctx.wait(sync, "vS")
                ctx.wait(sync, "dT0")
                ctx.inc("dY", sync.dma_start(out=dbg_t["vstg"],
                                             in_=vstage[1][:]), 16)
            block.sync(yvg)
        if "vsb" in dbg_t:
            def yvs(sync):
                ctx.wait(sync, "vS")
                ctx.wait(sync, "dT0")
                src = v_sb.rearrange("p (kt h e) -> p kt h e", kt=16, h=NH)[
                    :, :, :, 0:65]
                ctx.inc("dY", sync.dma_start(out=dbg_t["vsb"], in_=src), 16)
            block.sync(yvs)
        if "attn" in dbg_t:
            def yat(sync):
                ctx.wait(sync, "vS")
                for c in range(4):
                    ctx.inc("dY", sync.dma_start(
                        out=dbg_t["attn"][c * 128:(c + 1) * 128, :],
                        in_=attn[c][:]), 16)
            block.sync(yat)
        if "sums" in dbg_t:
            def ysm(sync):
                ctx.wait(sync, "sS")
                for h in range(NH):
                    ctx.inc("dY", sync.dma_start(
                        out=dbg_t["sums"][h:h + 1, :],
                        in_=sums_sb[h // 4][(h % 4) * 32:(h % 4) * 32 + 1, :]), 16)
            block.sync(ysm)

        snap = {n: s[1] for n, s in ctx.sems.items()}
        for e in ("tensor", "vector", "scalar", "gpsimd", "sync"):
            def bf2(eng, snap=snap):
                for n, v in snap.items():
                    if v > 0:
                        eng.wait_ge(ctx.sems[n][0], v)
            getattr(block, e)(bf2)
        p2.close()
        for b in range(8):
            bank_tok[b] = None

        p3 = ExitStack()
        wot = [p3.enter_context(nc.sbuf_tensor(f"wot{i}", [128, FQ], BF16))
               for i in range(2)]
        outst = [p3.enter_context(nc.sbuf_tensor(f"outst{i}", [128, TB], F32))
                 for i in range(4)]
        outst_cons = [None] * 4
        wot_cons = [None, None]
        wot_ld = {}

        def emit_wotload(pg, dm):
            if dm >= CH:
                return
            b = dm % 2
            def sf(s, pg=pg, dm=dm, b=b):
                ctx.wait_tok(s, wot_cons[b])
                ctx.inc("dT%d" % b, s.dma_start(out=wot[b][:], in_=woT_im[dm]), 16)
            block.scalar(sf)
            wot_ld[(pg, dm)] = ctx.tok("dT%d" % b)

        for pg in range(2):
            if pg == 0:
                x1_ld = x1_ld_pg0
            else:
                def yr(sync, pg=pg):
                    ctx.wait(sync, "tS")
                    ctx.wait(sync, "sS")
                    ctx.wait(sync, "vS")
                    for a in range(8):
                        dst = x1g.rearrange("p (dc w) -> p dc w", w=X1W)[
                            :, :, a * 129: a * 129 + 128]
                        src = x1_dram[:, pg * PG + a * 128:
                                      pg * PG + (a + 1) * 128].rearrange(
                            "(dc p) t -> p dc t", p=128)
                        ctx.inc("dY", sync.dma_start(out=dst, in_=src), 16)
                block.sync(yr)
                x1_ld = ctx.tok("dY")

            emit_wotload(pg, 0)
            emit_wotload(pg, 1)
            for dm in range(CH):
                pair = (0, 1) if dm % 2 == 0 else (2, 3)
                def tw2(t, pg=pg, dm=dm, pair=pair, x1_ld=x1_ld):
                    ctx.wait_tok(t, wot_ld[(pg, dm)])
                    ctx.wait_tok(t, bank_tok[pair[0]])
                    ctx.wait_tok(t, bank_tok[pair[1]])
                    for fc in range(4):
                        for js in range(2):
                            t.matmul(BK(pair[js]),
                                     wot[dm % 2][:, fc * 128:(fc + 1) * 128],
                                     attn[fc][:, pg * PG + js * 512:
                                               pg * PG + (js + 1) * 512],
                                     start=(fc == 0), stop=False)
                    ctx.wait_tok(t, x1_ld)
                    for js in range(2):
                        ctx.inc("tS", t.matmul(
                            BK(pair[js]), eyeb[:],
                            strided_tok(x1g, dm, X1W, js * 4, 4, 128),
                            start=False, stop=True))
                block.tensor(tw2)
                wot_cons[dm % 2] = ctx.tok("tS")
                emit_wotload(pg, dm + 2)

                def sw2(v, dm=dm, pair=pair):
                    ctx.wait(v, "tS")
                    dst = x1g[:, dm * X1W: dm * X1W + 8 * 129].rearrange(
                        "p (a b) -> p a b", b=129)[:, :, 0:128]
                    ctx.inc("vS", v.tensor_scalar_add(
                        dst, PS[0:128, pair[0] * 512: pair[0] * 512 + 1024]
                        .rearrange("p (a b) -> p a b", b=128),
                        wob_col[:, dm:dm + 1]))
                block.vector(sw2)
                bank_tok[pair[0]] = ctx.tok("vS")
                bank_tok[pair[1]] = ctx.tok("vS")

            if "x2" in dbg_t and pg == 0:
                pass

            for j in range(4):
                tb = pg * 4 + j

                def src_x2(dc, ti, n, pg=pg):
                    lo = dc * X1W + (ti - pg * 8) * 129
                    return x1g[:, lo: lo + n]

                def src_mm2(dc, j=j):
                    return strided_tok(x1g, dc, X1W, j * 2, 2, 128)

                def src_res2(oc, j=j):
                    return strided_tok(x1g, oc, X1W, j * 2, 2, 128)

                def pre(t):
                    ctx.wait(t, "vS")
                g3 = ln_gram(src_x2, tb * 2, pre_wait=pre if j == 0 else None)
                mid_main(tb, src_mm2)
                vtok = ln_statsV(tb * 2, g3)
                stok = ln_lnexp(tb * 2, vtok)
                rtok = ln_rowsT(tb * 2, stok)
                v_mrow = ln_rowsV(tb * 2, rtok)

                dO_base = [ctx.val("dO%d" % ob) for ob in range(4)]
                etoks = {}

                def evac_s(s, oc, bkk, dO_base=dO_base, etoks=etoks):
                    ob = oc % 4
                    want = dO_base[ob] + 16 * (oc // 4)
                    if want > 0:
                        ctx.wait(s, "dO%d" % ob, want)
                    ctx.inc("sS", s.copy(outst[ob][:], BK(bkk, 0, TB)))
                    etoks[oc] = ctx.tok("sS")

                def evac_v(v, oc, bkk, dO_base=dO_base, etoks=etoks):
                    ob = oc % 4
                    want = dO_base[ob] + 16 * (oc // 4)
                    if want > 0:
                        ctx.wait(v, "dO%d" % ob, want)
                    ctx.inc("vS", v.tensor_copy(outst[ob][:], BK(bkk, 0, TB)))
                    etoks[oc] = ctx.tok("vS")

                ffn_gate_delta(tb, src_res2, evac_s, evac_v, v_mrow)

                def yout(sync, tb=tb, etoks=etoks):
                    for oc in range(CH):
                        ctx.wait_tok(sync, etoks[oc])
                        ctx.inc("dO%d" % (oc % 4), sync.dma_start(
                            out=outT[oc * 128:(oc + 1) * 128,
                                     tb * TB:(tb + 1) * TB],
                            in_=outst[oc % 4][:]), 16)
                block.sync(yout)
        def yfin(sync):
            ctx.wait(sync, "dY")
            for ob in range(4):
                ctx.wait(sync, "dO%d" % ob)
        block.sync(yfin)
        p3.close()

    return nc


_NC_CACHE = {}


def _get_nc(dbg=()):
    key = tuple(sorted(dbg))
    if key not in _NC_CACHE:
        _NC_CACHE[key] = build_program(dbg=key)
    return _NC_CACHE[key]


def _tile_w(w):
    fo, d = w.shape
    return np.ascontiguousarray(
        w.reshape(fo // 128, 128, d // 128, 128).transpose(0, 3, 2, 1)
        .reshape(fo // 128, 128, d))


def make_inmaps(L0_attn_x, L0_attn_Wqk_w, L0_attn_Wqk_b, L0_attn_Wv_w,
                L0_attn_Wv_b, L0_attn_Wo_w, L0_attn_Wo_b, L0_ffn_router_w,
                L0_ffn_A, L0_ffn_B):
    x = np.asarray(L0_attn_x, np.float32)
    wqk = np.asarray(L0_attn_Wqk_w, np.float32)
    wqkb = np.asarray(L0_attn_Wqk_b, np.float32)
    aext = np.concatenate(
        [np.asarray(L0_ffn_A, np.float32).reshape(ER, D),
         np.asarray(L0_ffn_router_w, np.float32)], axis=0)
    atil = np.ascontiguousarray(
        aext.reshape(MW, CH, 128).transpose(2, 1, 0).reshape(128, CH * MW))
    btil = np.ascontiguousarray(
        np.asarray(L0_ffn_B, np.float32).transpose(0, 2, 1).reshape(ER, D))
    shared = {
        "qtil": _tile_w(wqk[0::2]),
        "ktil": _tile_w(wqk[1::2]),
        "vtil": _tile_w(np.asarray(L0_attn_Wv_w, np.float32)),
        "qb": np.ascontiguousarray(wqkb[0::2]),
        "kb": np.ascontiguousarray(wqkb[1::2]),
        "vb": np.ascontiguousarray(np.asarray(L0_attn_Wv_b, np.float32)),
        "wotil": _tile_w(np.asarray(L0_attn_Wo_w, np.float32)),
        "wob": np.ascontiguousarray(np.asarray(L0_attn_Wo_b, np.float32)),
        "atil": atil,
        "btil": btil,
    }
    return [dict(shared, xT=np.ascontiguousarray(x[b].T)) for b in range(8)]


def kernel(**inputs):
    in_maps = make_inmaps(**inputs)
    nc = _get_nc()
    res = run_bass_kernel_spmd(nc, in_maps, core_ids=list(range(8)))
    return np.stack([np.asarray(r["outT"]).T for r in res.results]).astype(np.float32)
